# version 20
# baseline (speedup 1.0000x reference)
import contextlib

import numpy as np
import ml_dtypes
import concourse.bass as bass
import concourse.mybir as mybir
from concourse.bass_utils import run_bass_kernel_spmd

F32 = mybir.dt.float32
BF16 = mybir.dt.bfloat16
AF = mybir.ActivationFunctionType
ALU = mybir.AluOpType

P = 128
NCORES = 8
TPC = 1024
G = 16
GA = 64
S8192 = float(np.float32(1.0) / np.float32(np.sqrt(np.float32(8192.0))))

VC = {n: i * 16 for i, n in enumerate(
    ["tmk", "tmv", "tmr", "bk", "bv", "br", "td", "tf", "bo"])}


class Sem:
    def __init__(self, h):
        self.h = h
        self.n = 0

    def nxt(self, d):
        self.n += d
        return self.n


def build():
    nc = bass.Bass("TRN2", target_bir_lowering=False)

    xT = nc.dram_tensor("xT", [2048, TPC + 1], F32, kind="ExternalInput").ap()
    wk = nc.dram_tensor("wk", [G, P, 2048], F32, kind="ExternalInput").ap()
    wv = nc.dram_tensor("wv", [G, P, 2048], F32, kind="ExternalInput").ap()
    wr = nc.dram_tensor("wr", [G, P, 2048], F32, kind="ExternalInput").ap()
    wo = nc.dram_tensor("wo", [G, P, 2048], F32, kind="ExternalInput").ap()
    w1 = nc.dram_tensor("w1", [GA, P, 2048], F32, kind="ExternalInput").ap()
    w3 = nc.dram_tensor("w3", [GA, P, 2048], F32, kind="ExternalInput").ap()
    w2 = nc.dram_tensor("w2", [GA, P, 2048], F32, kind="ExternalInput").ap()
    vecs = nc.dram_tensor("vecs", [P, 144], F32, kind="ExternalInput").ap()
    cpow = nc.dram_tensor("cpow", [P, 8], F32, kind="ExternalInput").ap()
    h128 = nc.dram_tensor("h128", [P, P], BF16, kind="ExternalInput").ap()
    out_fm = nc.dram_tensor("out_fm", [2048, TPC], F32, kind="ExternalOutput").ap()

    cc_in = nc.dram_tensor("cc_in", [2, P, 16], F32)
    cc_out = nc.dram_tensor("cc_out", [NCORES, 2, P, 16], F32)

    EP = {k: [] for k in ("sync", "tensor", "vector", "scalar", "gpsimd")}

    def E(eng, fn):
        EP[eng].append(fn)

    def W(eng, sem, tok):
        EP[eng].append(lambda e, s=sem.h, t=tok: e.wait_ge(s, t))

    def DMA(eng, sem, out, in_):
        tok = sem.nxt(16)
        EP[eng].append(lambda e, o=out, i=in_, s=sem.h:
                       e.dma_start(out=o, in_=i).then_inc(s, 16))
        return tok

    def OP(eng, sem, fn):
        tok = sem.nxt(1)
        EP[eng].append(lambda e, f=fn, s=sem.h: f(e).then_inc(s, 1))
        return tok

    with contextlib.ExitStack() as ctx:
        ent = ctx.enter_context
        XST = TPC + 2
        xn = ent(nc.sbuf_tensor("xn", [P, G * XST], BF16))
        b1 = ent(nc.sbuf_tensor("b1", [P, G * TPC], BF16))
        bigb = ent(nc.sbuf_tensor("bigb", [P, 2 * G * TPC], BF16))
        ssa = ent(nc.sbuf_tensor("ssa", [P, 2 * (TPC + 1)], F32))
        smm = ent(nc.sbuf_tensor("smm", [P, 2 * (TPC + 1)], F32))
        wf = [ent(nc.sbuf_tensor(f"wf{i}", [P, 2048], F32)) for i in range(2)]
        wb = [ent(nc.sbuf_tensor(f"wb{i}", [P, 2048], BF16)) for i in range(4)]
        vecs_s = ent(nc.sbuf_tensor("vecs_s", [P, 144], F32))
        cpow_s = ent(nc.sbuf_tensor("cpow_s", [P, 8], F32))
        h128_s = ent(nc.sbuf_tensor("h128_s", [P, P], BF16))
        dcol = ent(nc.sbuf_tensor("dcol", [P, 144], F32))
        gab = ent(nc.sbuf_tensor("gab", [P, 256], F32))
        lae = ent(nc.sbuf_tensor("lae", [P, 64], F32))
        ctab = ent(nc.sbuf_tensor("ctab", [P, 128], F32))
        ones1 = ent(nc.sbuf_tensor("ones1", [P, 1], F32))
        ones1k = ent(nc.sbuf_tensor("ones1k", [P, TPC], F32))
        lamp1 = ent(nc.sbuf_tensor("lamp1", [P, TPC], F32))
        onesr = ent(nc.sbuf_tensor("onesr", [1, P], F32))

        stats_ctx = contextlib.ExitStack()
        stat_m = stats_ctx.enter_context(nc.sbuf_tensor("stat_m", [1, TPC + 1], F32))
        stat_q = stats_ctx.enter_context(nc.sbuf_tensor("stat_q", [1, TPC + 1], F32))
        stat_t = stats_ctx.enter_context(nc.sbuf_tensor("stat_t", [1, TPC + 1], F32))

        p0 = ent(nc.psum_tensor("p0", [P, 1024], F32))
        p1 = ent(nc.psum_tensor("p1", [P, 1024], F32))
        p2 = ent(nc.psum_tensor("p2", [P, 1024], F32))
        p3 = ent(nc.psum_tensor("p3", [P, 1024], F32))
        PS = [p0, p1, p2, p3]

        s_w = Sem(ent(nc.semaphore("s_w")))
        s_g = Sem(ent(nc.semaphore("s_g")))
        s_mm = Sem(ent(nc.semaphore("s_mm")))
        s_v = Sem(ent(nc.semaphore("s_v")))
        s_a = Sem(ent(nc.semaphore("s_a")))
        s_cc = Sem(ent(nc.semaphore("s_cc")))
        s_st = Sem(ent(nc.semaphore("s_st")))

        def xn_own(g):
            return xn[:, g * XST + 2:(g + 1) * XST]

        def xn_prev(g):
            return xn[:, g * XST + 1:(g + 1) * XST - 1]

        def b1t(g):
            return b1[:, g * TPC:(g + 1) * TPC]

        def bigL(g):
            return bigb[:, g * TPC:(g + 1) * TPC]

        def bigR(g):
            return bigb[:, (G + g) * TPC:(G + g + 1) * TPC]

        def vcol(name, g):
            return vecs_s[:, VC[name] + g:VC[name] + g + 1]

        d_decay = dcol[:, 0:16]
        d_lam = dcol[:, 16:32]
        d_cu = dcol[:, 32:48]
        d_ct1 = dcol[:, 48:64]
        d_ct2 = dcol[:, 64:80]
        d_om = {"tmk": dcol[:, 80:96], "tmv": dcol[:, 96:112],
                "tmr": dcol[:, 112:128]}

        class WStream:
            def __init__(self):
                self.i = 0
                self.dma_tok = {}
                self.cast_tok = {}
                self.cons_tok = {}

            def load(self, src):
                i = self.i
                self.i += 1
                if i >= 2:
                    W("sync", s_a, self.cast_tok[i - 2])
                self.dma_tok[i] = DMA("sync", s_w, wf[i % 2][:, :], src)
                return i

            def cast(self, i):
                W("scalar", s_w, self.dma_tok[i])
                if i >= 4:
                    W("scalar", s_mm, self.cons_tok[i - 4])
                self.cast_tok[i] = OP(
                    "scalar", s_a,
                    lambda e, i=i: e.copy(wb[i % 4][:, :], wf[i % 2][:, :]))
                return self.cast_tok[i]

            def consumed(self, i, tok):
                self.cons_tok[i] = tok

        ws = WStream()

        t_vecs = DMA("gpsimd", s_g, vecs_s[:, :], vecs)
        t_cpow = DMA("gpsimd", s_g, cpow_s[:, :], cpow)
        t_h128 = DMA("gpsimd", s_g, h128_s[:, :], h128)

        E("vector", lambda e: e.memset(ones1[:, :], 1.0))
        E("vector", lambda e: e.memset(ones1k[:, :], 1.0))
        t_ones = OP("vector", s_v, lambda e: e.memset(onesr[:, :], 1.0))
        W("vector", s_g, t_vecs)
        for nm in ("tmk", "tmv"):
            E("vector", lambda e, nm=nm: e.tensor_scalar(
                d_om[nm], vecs_s[:, VC[nm]:VC[nm] + 16], -1.0, 1.0,
                ALU.mult, ALU.add))
        t_om = OP("vector", s_v, lambda e: e.tensor_scalar(
            d_om["tmr"], vecs_s[:, VC["tmr"]:VC["tmr"] + 16], -1.0, 1.0,
            ALU.mult, ALU.add))

        W("scalar", s_g, t_vecs)
        E("scalar", lambda e: e.activation(
            d_ct1, vecs_s[:, VC["td"]:VC["td"] + 16], AF.Exp))
        E("scalar", lambda e: e.drain())
        E("scalar", lambda e: e.mul(d_decay, d_ct1, -1.0))
        E("scalar", lambda e: e.drain())
        E("scalar", lambda e: e.activation(d_lam, d_decay, AF.Exp))
        E("scalar", lambda e: e.drain())
        t_dvec = OP("scalar", s_a, lambda e: e.activation(
            d_cu, vecs_s[:, VC["tf"]:VC["tf"] + 16], AF.Exp))

        Xst = [ssa[:, 0:TPC + 1], ssa[:, TPC + 1:2 * (TPC + 1)]]
        Qst = [smm[:, 0:TPC + 1], smm[:, TPC + 1:2 * (TPC + 1)]]
        t_xld = {}
        t_xsq = {}
        t_mmstat = {}
        for g in range(G):
            if g >= 2:
                W("gpsimd", s_mm, t_mmstat[g - 2])
            t_xld[g] = DMA("gpsimd", s_g, Xst[g % 2][:, :],
                           xT[g * P:(g + 1) * P, :])
            W("scalar", s_g, t_xld[g])
            if g >= 2:
                W("scalar", s_mm, t_mmstat[g - 2])
            t_xsq[g] = OP("scalar", s_a, lambda e, g=g: e.activation(
                Qst[g % 2][:, :], Xst[g % 2][:, :], AF.Square))
            if g == 0:
                W("tensor", s_v, t_ones)
            W("tensor", s_a, t_xsq[g])
            st, sp = (g == 0), (g == G - 1)
            for reg, src in [
                (p0[0:1, 0:512], lambda gg: Xst[gg % 2][:, 0:512]),
                (p0[0:1, 512:1024], lambda gg: Xst[gg % 2][:, 512:1024]),
                (p1[0:1, 0:1], lambda gg: Xst[gg % 2][:, 1024:1025]),
                (p2[0:1, 0:512], lambda gg: Qst[gg % 2][:, 0:512]),
                (p2[0:1, 512:1024], lambda gg: Qst[gg % 2][:, 512:1024]),
            ]:
                E("tensor", lambda e, g=g, st=st, sp=sp, reg=reg, src=src:
                  e.matmul(reg, ones1[:, :], src(g), start=st, stop=sp,
                           skip_group_check=True))
            t_mmstat[g] = OP("tensor", s_mm, lambda e, g=g, st=st, sp=sp:
                             e.matmul(p3[0:1, 0:1], ones1[:, :],
                                      Qst[g % 2][:, 1024:1025],
                                      start=st, stop=sp, skip_group_check=True))

        W("scalar", s_mm, t_mmstat[G - 1])
        E("scalar", lambda e: e.activation(stat_m[0:1, 0:1024], p0[0:1, 0:1024],
                                           AF.Identity, bias=0.0, scale=1.0 / 2048))
        E("scalar", lambda e: e.activation(stat_m[0:1, 1024:1025], p1[0:1, 0:1],
                                           AF.Identity, bias=0.0, scale=1.0 / 2048))
        E("scalar", lambda e: e.activation(stat_q[0:1, 0:1024], p2[0:1, 0:1024],
                                           AF.Identity, bias=0.0, scale=1.0 / 2048))
        t_statev = OP("scalar", s_a, lambda e: e.activation(
            stat_q[0:1, 1024:1025], p3[0:1, 0:1], AF.Identity,
            bias=0.0, scale=1.0 / 2048))
        W("vector", s_a, t_statev)
        E("vector", lambda e: e.tensor_tensor(
            stat_t[0:1, :], stat_m[0:1, :], stat_m[0:1, :], ALU.mult))
        E("vector", lambda e: e.tensor_tensor(
            stat_t[0:1, :], stat_q[0:1, :], stat_t[0:1, :], ALU.subtract))
        t_var = OP("vector", s_v, lambda e: e.tensor_scalar_add(
            stat_t[0:1, :], stat_t[0:1, :], 1e-5))
        W("scalar", s_v, t_var)
        E("scalar", lambda e: e.activation(stat_t[0:1, :], stat_t[0:1, :], AF.Ln))
        t_rstd = OP("scalar", s_a, lambda e: e.activation(
            stat_t[0:1, :], stat_t[0:1, :], AF.Exp, bias=0.0, scale=-0.5))
        W("vector", s_a, t_rstd)
        t_mr = OP("vector", s_v, lambda e: e.tensor_tensor(
            stat_m[0:1, :], stat_m[0:1, :], stat_t[0:1, :], ALU.mult))

        W("tensor", s_v, t_mr)
        W("tensor", s_a, t_statev)
        for reg, src in [
            (p0[:, 0:512], stat_t[0:1, 0:512]),
            (p0[:, 512:1024], stat_t[0:1, 512:1024]),
            (p1[:, 0:1], stat_t[0:1, 1024:1025]),
            (p2[:, 0:512], stat_m[0:1, 0:512]),
            (p2[:, 512:1024], stat_m[0:1, 512:1024]),
        ]:
            E("tensor", lambda e, reg=reg, src=src: e.matmul(
                reg, onesr[:, :], src, start=True, stop=True,
                skip_group_check=True))
        t_mmrep = OP("tensor", s_mm, lambda e: e.matmul(
            p3[:, 0:1], onesr[:, :], stat_m[0:1, 1024:1025],
            start=True, stop=True, skip_group_check=True))
        W("scalar", s_mm, t_mmrep)
        E("scalar", lambda e: e.activation(smm[:, 0:1024], p0[:, 0:1024],
                                           AF.Identity, bias=0.0, scale=1.0))
        E("scalar", lambda e: e.activation(smm[:, 1024:1025], p1[:, 0:1],
                                           AF.Identity, bias=0.0, scale=1.0))
        E("scalar", lambda e: e.activation(smm[:, 1025:2049], p2[:, 0:1024],
                                           AF.Identity, bias=0.0, scale=1.0))
        t_rep = OP("scalar", s_a, lambda e: e.activation(
            smm[:, 2049:2050], p3[:, 0:1], AF.Identity, bias=0.0, scale=1.0))
        rstd_rep = smm[:, 0:TPC + 1]
        mr_rep = smm[:, TPC + 1:2 * (TPC + 1)]

        t_xn = {}
        for g in range(G):
            if g >= 2:
                W("gpsimd", s_v, t_xn[g - 2])
            else:
                W("gpsimd", s_mm, t_mmstat[G - 2 + g])
            tl = DMA("gpsimd", s_g, Xst[g % 2][:, :], xT[g * P:(g + 1) * P, :])
            W("vector", s_g, tl)
            if g == 0:
                W("vector", s_a, t_rep)
            E("vector", lambda e, g=g: e.tensor_tensor(
                Xst[g % 2][:, :], Xst[g % 2][:, :], rstd_rep, ALU.mult))
            t_xn[g] = OP("vector", s_v, lambda e, g=g: e.tensor_tensor(
                xn[:, g * XST + 1:(g + 1) * XST], Xst[g % 2][:, :],
                mr_rep, ALU.subtract))

        def lerp(dst_fn, tm_name):
            toks = {}
            for g in range(G):
                E("vector", lambda e, g=g: e.tensor_scalar_mul(
                    dst_fn(g), xn_own(g), vcol(tm_name, g)))
                toks[g] = OP("vector", s_v, lambda e, g=g: e.scalar_tensor_tensor(
                    dst_fn(g), xn_prev(g), d_om[tm_name][:, g:g + 1], dst_fn(g),
                    ALU.mult, ALU.add))
            return toks

        t_xk = lerp(b1t, "tmk")

        def proj_mms(i, ps, rhs_fn, tok_sem=True):
            t = None
            for dc in range(G):
                st, sp = (dc == 0), (dc == G - 1)
                for half in range(2):
                    fn = (lambda e, i=i, dc=dc, st=st, sp=sp, ps=ps, half=half:
                          e.matmul(ps[:, half * 512:(half + 1) * 512],
                                   wb[i % 4][:, dc * 128:(dc + 1) * 128],
                                   rhs_fn(dc, half), start=st, stop=sp,
                                   skip_group_check=True))
                    if dc == G - 1 and half == 1:
                        t = OP("tensor", s_mm, fn)
                    else:
                        E("tensor", fn)
            return t

        def b1_rhs(dc, half):
            return b1[:, dc * TPC + half * 512:dc * TPC + (half + 1) * 512]

        def bigR_rhs(dc, half):
            return bigb[:, (G + dc) * TPC + half * 512:
                        (G + dc) * TPC + (half + 1) * 512]

        t_kmm = {}
        t_kev = {}
        for ct in range(G):
            i = ws.load(wk[ct])
            if ct >= 1:
                W("scalar", s_mm, t_kmm[ct - 1])
                t_kev[ct - 1] = OP("scalar", s_a, lambda e, ct=ct - 1: e.activation(
                    bigL(ct), PS[ct % 4][:, :], AF.Exp,
                    bias=vcol("bk", ct), scale=1.0))
            ws.cast(i)
            W("tensor", s_a, ws.cast_tok[i])
            if ct == 0:
                W("tensor", s_v, t_xk[G - 1])
                W("tensor", s_a, t_rep)
            if ct >= 4:
                W("tensor", s_a, t_kev[ct - 4])
            t_kmm[ct] = proj_mms(i, PS[ct % 4], b1_rhs)
            ws.consumed(i, t_kmm[ct])
        W("scalar", s_mm, t_kmm[G - 1])
        t_kev[G - 1] = OP("scalar", s_a, lambda e, ct=G - 1: e.activation(
            bigL(ct), PS[ct % 4][:, :], AF.Exp, bias=vcol("bk", ct), scale=1.0))

        t_xv = lerp(bigR, "tmv")

        SA2 = ssa[:, 0:TPC]
        SB2 = ssa[:, TPC:2 * TPC]
        SCR = smm[:, 0:TPC]
        lam_rep = smm[:, TPC:2 * TPC]

        vtmp = [smm[:, 0:TPC], smm[:, TPC:2 * TPC]]
        t_vmm = {}
        t_vev = {}
        t_ekv = {}
        t_p1 = None
        for ct in range(G):
            i = ws.load(wv[ct])
            if ct >= 1:
                c = ct - 1
                W("scalar", s_mm, t_vmm[c])
                if c == 0:
                    W("scalar", s_v, t_xn[G - 1])
                if c >= 2:
                    W("scalar", s_v, t_ekv[c - 2])
                t_vev[c] = OP("scalar", s_a, lambda e, c=c: e.activation(
                    vtmp[c % 2][:, :], PS[c % 4][:, :], AF.Identity,
                    bias=vcol("bv", c), scale=1.0))
                W("vector", s_a, t_vev[c])
                if c == 0:
                    W("vector", s_mm, t_kmm[G - 1])
                t_ekv[c] = OP("vector", s_v, lambda e, c=c: e.tensor_tensor(
                    b1t(c), bigL(c), vtmp[c % 2][:, :], ALU.mult))
                if c == 0:
                    W("vector", s_a, t_dvec)
                E("vector", lambda e, c=c: e.tensor_scalar_mul(
                    lamp1[:, :], ones1k[:, :], d_lam[:, c:c + 1]))
                E("vector", lambda e, c=c: e.tensor_tensor_scan(
                    SA2, lamp1[:, :], b1t(c), 0.0, ALU.mult, ALU.add))
                E("vector", lambda e, c=c: e.tensor_tensor_scan(
                    SB2, lamp1[:, :], bigL(c), 0.0, ALU.mult, ALU.add))
                E("vector", lambda e: e.drain())
                E("vector", lambda e, c=c: e.tensor_copy(
                    lae[:, c:c + 1], SA2[:, TPC - 1:TPC]))
                t_p1 = OP("vector", s_v, lambda e, c=c: e.tensor_copy(
                    lae[:, 16 + c:17 + c], SB2[:, TPC - 1:TPC]))
            ws.cast(i)
            W("tensor", s_a, ws.cast_tok[i])
            if ct == 0:
                W("tensor", s_v, t_xv[G - 1])
                W("tensor", s_a, t_kev[G - 1])
            if ct >= 4:
                W("tensor", s_a, t_vev[ct - 4])
            t_vmm[ct] = proj_mms(i, PS[ct % 4], bigR_rhs)
            ws.consumed(i, t_vmm[ct])
        c = G - 1
        W("scalar", s_mm, t_vmm[c])
        W("scalar", s_v, t_ekv[c - 2])
        t_vev[c] = OP("scalar", s_a, lambda e, c=c: e.activation(
            vtmp[c % 2][:, :], PS[c % 4][:, :], AF.Identity,
            bias=vcol("bv", c), scale=1.0))
        W("vector", s_a, t_vev[c])
        t_ekv[c] = OP("vector", s_v, lambda e, c=c: e.tensor_tensor(
            b1t(c), bigL(c), vtmp[c % 2][:, :], ALU.mult))
        E("vector", lambda e, c=c: e.tensor_scalar_mul(
            lamp1[:, :], ones1k[:, :], d_lam[:, c:c + 1]))
        E("vector", lambda e, c=c: e.tensor_tensor_scan(
            SA2, lamp1[:, :], b1t(c), 0.0, ALU.mult, ALU.add))
        E("vector", lambda e, c=c: e.tensor_tensor_scan(
            SB2, lamp1[:, :], bigL(c), 0.0, ALU.mult, ALU.add))
        E("vector", lambda e: e.drain())
        E("vector", lambda e, c=c: e.tensor_copy(
            lae[:, c:c + 1], SA2[:, TPC - 1:TPC]))
        t_p1 = OP("vector", s_v, lambda e, c=c: e.tensor_copy(
            lae[:, 16 + c:17 + c], SB2[:, TPC - 1:TPC]))

        W("gpsimd", s_v, t_p1)
        DMA("gpsimd", s_g, cc_in[0], lae[:, 0:16])
        t_ccin = DMA("gpsimd", s_g, cc_in[1], lae[:, 16:32])
        W("gpsimd", s_g, t_ccin)
        t_ccdone = OP("gpsimd", s_cc, lambda e: e.collective_compute(
            "AllGather", ALU.bypass, replica_groups=[list(range(NCORES))],
            ins=[cc_in.ap().opt()], outs=[cc_out.ap().opt()]))
        W("gpsimd", s_cc, t_ccdone)
        t_gab = None
        for c in range(NCORES):
            DMA("gpsimd", s_g, gab[:, c * 16:(c + 1) * 16], cc_out[c, 0])
            t_gab = DMA("gpsimd", s_g, gab[:, 128 + c * 16:128 + (c + 1) * 16],
                        cc_out[c, 1])

        W("vector", s_g, t_gab)
        W("vector", s_g, t_cpow)
        for c in range(NCORES - 1):
            E("vector", lambda e, c=c: e.tensor_scalar_mul(
                ctab[:, c * 16:(c + 1) * 16], d_decay, cpow_s[:, c:c + 1]))
        t_ctab = OP("vector", s_v, lambda e, c=NCORES - 1: e.tensor_scalar_mul(
            ctab[:, c * 16:(c + 1) * 16], d_decay, cpow_s[:, c:c + 1]))
        W("scalar", s_v, t_ctab)
        t_cexp = OP("scalar", s_a,
                    lambda e: e.activation(ctab[:, :], ctab[:, :], AF.Exp))
        W("vector", s_a, t_cexp)
        E("vector", lambda e: e.tensor_tensor(
            gab[:, 0:128], ctab[:, :], gab[:, 0:128], ALU.mult))
        E("vector", lambda e: e.tensor_tensor(
            gab[:, 128:256], ctab[:, :], gab[:, 128:256], ALU.mult))
        E("vector", lambda e: e.drain())
        E("vector", lambda e: e.tensor_reduce(
            lae[:, 32:48],
            gab[:, 0:128].rearrange("p (c g) -> p g c", c=8, g=16),
            mybir.AxisListType.X, ALU.add))
        E("vector", lambda e: e.drain())
        t_astart = OP("vector", s_v, lambda e: e.tensor_reduce(
            lae[:, 48:64],
            gab[:, 128:256].rearrange("p (c g) -> p g c", c=8, g=16),
            mybir.AxisListType.X, ALU.add))

        W("vector", s_mm, t_vmm[G - 1])
        t_xr = lerp(bigR, "tmr")

        stats_ctx.close()
        sgb = ent(nc.sbuf_tensor("sgb", [P, 2048], BF16))

        t_rmm = {}
        t_nd = {}
        t_recip = {}
        t_wkv = {}
        t_sig = {}
        t_rwkv = {}
        for g in range(G):
            i = ws.load(wr[g])
            ws.cast(i)
            W("tensor", s_a, ws.cast_tok[i])
            if g == 0:
                W("tensor", s_v, t_xr[G - 1])
                W("tensor", s_a, t_vev[G - 1])
            if g >= 4:
                W("tensor", s_a, t_sig[g - 4])
            t_rmm[g] = proj_mms(i, PS[g % 4], bigR_rhs)
            ws.consumed(i, t_rmm[g])

            if g == 0:
                W("vector", s_v, t_astart)
            E("vector", lambda e, g=g: e.tensor_scalar_mul(
                lam_rep, ones1k[:, :], d_lam[:, g:g + 1]))
            E("vector", lambda e, g=g: e.tensor_tensor_scan(
                SCR, lam_rep, b1t(g), lae[:, 32 + g:33 + g],
                ALU.mult, ALU.add))
            E("vector", lambda e, g=g: e.scalar_tensor_tensor(
                SA2[:, 1:TPC], b1t(g)[:, 1:TPC], d_cu[:, g:g + 1],
                SCR[:, 0:TPC - 1], ALU.mult, ALU.add))
            E("vector", lambda e, g=g: e.scalar_tensor_tensor(
                SA2[:, 0:1], b1t(g)[:, 0:1], d_cu[:, g:g + 1],
                lae[:, 32 + g:33 + g], ALU.mult, ALU.add))
            E("vector", lambda e, g=g: e.tensor_tensor_scan(
                SCR, lam_rep, bigL(g), lae[:, 48 + g:49 + g],
                ALU.mult, ALU.add))
            E("vector", lambda e, g=g: e.scalar_tensor_tensor(
                SB2[:, 1:TPC], bigL(g)[:, 1:TPC], d_cu[:, g:g + 1],
                SCR[:, 0:TPC - 1], ALU.mult, ALU.add))
            t_nd[g] = OP("vector", s_v, lambda e, g=g: e.scalar_tensor_tensor(
                SB2[:, 0:1], bigL(g)[:, 0:1], d_cu[:, g:g + 1],
                lae[:, 48 + g:49 + g], ALU.mult, ALU.add))
            W("scalar", s_v, t_nd[g])
            E("scalar", lambda e: e.activation(SB2, SB2, AF.Ln))
            t_recip[g] = OP("scalar", s_a, lambda e: e.activation(
                SB2, SB2, AF.Exp, bias=0.0, scale=-1.0))
            W("vector", s_a, t_recip[g])
            t_wkv[g] = OP("vector", s_v, lambda e, g=g: e.tensor_tensor(
                bigL(g), SA2, SB2, ALU.mult))
            W("scalar", s_mm, t_rmm[g])
            if g >= 2:
                W("scalar", s_v, t_rwkv[g - 2])
            t_sig[g] = OP("scalar", s_a, lambda e, g=g: e.activation(
                sgb[:, (g % 2) * 1024:(g % 2 + 1) * 1024], PS[g % 4][:, :],
                AF.Sigmoid, bias=vcol("br", g), scale=1.0))
            W("vector", s_a, t_sig[g])
            t_rwkv[g] = OP("vector", s_v, lambda e, g=g: e.tensor_tensor(
                bigL(g), sgb[:, (g % 2) * 1024:(g % 2 + 1) * 1024],
                bigL(g), ALU.mult))

        t_womm = {}
        t_woev = {}

        def wo_evac(dt):
            return lambda e, dt=dt: e.activation(
                b1t(dt), PS[dt % 4][:, :], AF.Identity,
                bias=vcol("bo", dt), scale=1.0)

        for dt in range(G):
            i = ws.load(wo[dt])
            if dt >= 1:
                W("scalar", s_mm, t_womm[dt - 1])
                if dt == 1:
                    W("scalar", s_v, t_nd[G - 1])
                t_woev[dt - 1] = OP("scalar", s_a, wo_evac(dt - 1))
            ws.cast(i)
            W("tensor", s_a, ws.cast_tok[i])
            if dt == 0:
                W("tensor", s_v, t_rwkv[G - 1])
                W("tensor", s_a, t_sig[G - 1])
            if dt >= 4:
                W("tensor", s_a, t_woev[dt - 4])
            t = None
            for cc in range(G):
                st, sp = (cc == 0), (cc == G - 1)
                for half in range(2):
                    fn = (lambda e, i=i, cc=cc, st=st, sp=sp, dt=dt, half=half:
                          e.matmul(PS[dt % 4][:, half * 512:(half + 1) * 512],
                                   wb[i % 4][:, cc * 128:(cc + 1) * 128],
                                   bigb[:, cc * TPC + half * 512:
                                        cc * TPC + (half + 1) * 512],
                                   start=st, stop=sp, skip_group_check=True))
                    if cc == G - 1 and half == 1:
                        t = OP("tensor", s_mm, fn)
                    else:
                        E("tensor", fn)
            t_womm[dt] = t
            ws.consumed(i, t)
        W("scalar", s_mm, t_womm[G - 1])
        t_woev[G - 1] = OP("scalar", s_a, wo_evac(G - 1))

        t_st_l = {}
        t_oev = {}
        t_w2mm_last = None
        t_hid_prev = None

        def bf_stage_pieces(h, lo_tile, n_tiles):
            nb = n_tiles // (2 * h)
            seg = bigb[:, lo_tile * 512:(lo_tile + n_tiles) * 512]
            ap5 = seg.rearrange("p (b two i t) -> p b two i t",
                                b=nb, two=2, i=h, t=512)
            out = []
            if h == 1:
                step = 2 if nb >= 2 else 1
                for b0 in range(0, nb, step):
                    bs = min(step, nb - b0)
                    out.append((ap5[:, b0:b0 + bs, 0], ap5[:, b0:b0 + bs, 1],
                                (bs, 1, 512)))
            else:
                ic = 2 if h >= 2 else 1
                for b in range(nb):
                    for i0 in range(0, h, ic):
                        out.append((ap5[:, b:b + 1, 0, i0:i0 + ic],
                                    ap5[:, b:b + 1, 1, i0:i0 + ic],
                                    (1, ic, 512)))
            return out

        bf_state = {"j": 0, "cp": {}, "last_cp": 0}

        def bf_emit(pieces_list):
            for (A, B, shp) in pieces_list:
                j = bf_state["j"]
                bf_state["j"] = j + 1
                half = sgb[:, (j % 2) * 1024:(j % 2 + 1) * 1024]
                halfr = half.rearrange("p (b i t) -> p b i t",
                                       b=shp[0], i=shp[1], t=shp[2])
                if j >= 2:
                    W("vector", s_a, bf_state["cp"][j - 2])
                E("vector", lambda e, A=A, B=B, hr=halfr: e.tensor_tensor(
                    hr, A, B, ALU.subtract))
                ta = OP("vector", s_v, lambda e, A=A, B=B: e.tensor_tensor(
                    A, A, B, ALU.add))
                W("scalar", s_v, ta)
                tcp = OP("scalar", s_a, lambda e, B=B, hr=halfr: e.copy(B, hr))
                bf_state["cp"][j] = tcp
                bf_state["last_cp"] = tcp

        for tc in range(2):
            t_ffmm = {}
            t_hid = {}
            t_silu = {}
            for a in range(GA):
                i1 = ws.load(w1[a])
                ws.cast(i1)
                i3 = ws.load(w3[a])
                ws.cast(i3)
                W("tensor", s_a, ws.cast_tok[i3])
                if a == 0 and tc == 0:
                    W("tensor", s_a, t_woev[G - 1])
                if a >= 4:
                    W("tensor", s_v, t_hid[a - 4])
                elif tc == 1:
                    W("tensor", s_v, t_hid_prev[GA - 4 + a])
                ps = PS[a % 4]
                t = None
                for dc in range(G):
                    st, sp = (dc == 0), (dc == G - 1)
                    rhs = xn[:, dc * XST + 2 + tc * 512:
                             dc * XST + 2 + tc * 512 + 512]
                    E("tensor", lambda e, i=i1, st=st, sp=sp, ps=ps, rhs=rhs, dc=dc:
                      e.matmul(ps[:, 0:512], wb[i % 4][:, dc * 128:(dc + 1) * 128],
                               rhs, start=st, stop=sp, skip_group_check=True))
                    fn = (lambda e, i=i3, st=st, sp=sp, ps=ps, rhs=rhs, dc=dc:
                          e.matmul(ps[:, 512:1024],
                                   wb[i % 4][:, dc * 128:(dc + 1) * 128],
                                   rhs, start=st, stop=sp, skip_group_check=True))
                    if dc == G - 1:
                        t = OP("tensor", s_mm, fn)
                    else:
                        E("tensor", fn)
                t_ffmm[a] = t
                ws.consumed(i1, t)
                ws.consumed(i3, t)
                W("scalar", s_mm, t_ffmm[a])
                if tc == 0 and a == 0:
                    W("scalar", s_v, t_wkv[G - 1])
                if a >= 2:
                    W("scalar", s_v, t_hid[a - 2])
                t_silu[a] = OP("scalar", s_a, lambda e, a=a, ps=ps: e.activation(
                    smm[:, (a % 2) * 512:(a % 2 + 1) * 512], ps[:, 0:512], AF.Silu))
                W("vector", s_a, t_silu[a])
                if tc == 0 and a == 0:
                    W("vector", s_v, t_nd[G - 1])
                    W("vector", s_mm, t_womm[G - 1])
                if tc == 1 and a == 0:
                    W("vector", s_mm, t_w2mm_last)
                t_hid[a] = OP("vector", s_v, lambda e, a=a, ps=ps: e.tensor_tensor(
                    bigb[:, a * 512:(a + 1) * 512],
                    smm[:, (a % 2) * 512:(a % 2 + 1) * 512],
                    ps[:, 512:1024], ALU.mult))
                if a % 16 == 15:
                    gb = a // 16
                    for h in (1, 2, 4, 8):
                        bf_emit(bf_stage_pieces(h, 16 * gb, 16))
            for h in (16, 32):
                bf_emit(bf_stage_pieces(h, 0, 64))
            t_hid_prev = t_hid
            t_bfly = bf_state["last_cp"]

            t_hmm = {}
            t_zev = {}
            for a in range(GA):
                if a == 0:
                    W("tensor", s_a, t_bfly)
                    W("tensor", s_g, t_h128)
                if a >= 2:
                    W("tensor", s_a, t_zev[a - 2])
                reg = p2[:, (a % 2) * 512:(a % 2 + 1) * 512]
                t_hmm[a] = OP("tensor", s_mm, lambda e, a=a, reg=reg: e.matmul(
                    reg, h128_s[:, :], bigb[:, a * 512:(a + 1) * 512],
                    start=True, stop=True, skip_group_check=True))
                W("scalar", s_mm, t_hmm[a])
                t_zev[a] = OP("scalar", s_a, lambda e, a=a, reg=reg: e.activation(
                    bigb[:, a * 512:(a + 1) * 512], reg, AF.Copy,
                    bias=0.0, scale=S8192))

            for dt in range(G):
                xsl = smm[:, 1024 + (dt % 2) * 512:1024 + (dt % 2 + 1) * 512]
                prev = None
                if dt >= 2:
                    prev = (tc, dt - 2)
                elif tc == 1:
                    prev = (0, G - 2 + dt)
                if prev is not None:
                    W("gpsimd", s_v, t_oev[prev])
                elif tc == 0:
                    W("gpsimd", s_v, t_nd[G - 1])
                t_xsl = DMA("gpsimd", s_g, xsl,
                            xT[dt * P:(dt + 1) * P,
                               1 + tc * 512:1 + tc * 512 + 512])
                t_w2mm = None
                for blk in range(4):
                    i = ws.load(w2[dt * 4 + blk])
                    ws.cast(i)
                    W("tensor", s_a, ws.cast_tok[i])
                    if blk == 0:
                        if dt == 0:
                            W("tensor", s_a, t_zev[GA - 1])
                        if prev is not None:
                            W("tensor", s_v, t_oev[prev])
                    reg = p3[:, (dt % 2) * 512:(dt % 2 + 1) * 512]
                    tb = None
                    for b16 in range(16):
                        b = blk * 16 + b16
                        st, sp = (b == 0), (b == GA - 1)
                        fn = (lambda e, i=i, b=b, b16=b16, st=st, sp=sp, reg=reg:
                              e.matmul(reg, wb[i % 4][:, b16 * 128:(b16 + 1) * 128],
                                       bigb[:, b * 512:(b + 1) * 512],
                                       start=st, stop=sp, skip_group_check=True))
                        if b16 == 15:
                            tb = OP("tensor", s_mm, fn)
                        else:
                            E("tensor", fn)
                    ws.consumed(i, tb)
                    if blk == 3:
                        t_w2mm = tb
                t_w2mm_last = t_w2mm
                evst = ssa[:, (dt % 2) * 512:(dt % 2 + 1) * 512]
                W("vector", s_mm, t_w2mm)
                W("vector", s_g, t_xsl)
                if prev is not None:
                    W("vector", s_st, t_st_l[prev])
                E("vector", lambda e, dt=dt, tc=tc, evst=evst: e.tensor_tensor(
                    evst, p3[:, (dt % 2) * 512:(dt % 2 + 1) * 512],
                    b1[:, dt * TPC + tc * 512:dt * TPC + tc * 512 + 512],
                    ALU.add))
                t_oev[(tc, dt)] = OP("vector", s_v, lambda e, evst=evst, xsl=xsl:
                                     e.tensor_tensor(evst, evst, xsl, ALU.add))
                W("gpsimd", s_v, t_oev[(tc, dt)])
                t_st_l[(tc, dt)] = DMA(
                    "gpsimd", s_st,
                    out_fm[dt * P:(dt + 1) * P, tc * 512:(tc + 1) * 512], evst)

        W("gpsimd", s_st, s_st.n)
        W("sync", s_st, s_st.n)

        block = ent(nc.Block())

        @block.sync
        def _(e):
            for f in EP["sync"]:
                f(e)

        @block.tensor
        def _(e):
            for f in EP["tensor"]:
                f(e)

        @block.vector
        def _(e):
            for f in EP["vector"]:
                f(e)

        @block.scalar
        def _(e):
            for f in EP["scalar"]:
                f(e)

        @block.gpsimd
        def _(e):
            for f in EP["gpsimd"]:
                f(e)

    return nc


def _tile_proj(wm):
    o, h = wm.shape
    return np.ascontiguousarray(
        wm.reshape(o // 128, 128, h // 128, 128).transpose(0, 3, 2, 1)
    ).reshape(o // 128, 128, h)


def _tile_w2(wm):
    return np.ascontiguousarray(
        wm.reshape(16, 128, 4, 16, 128).transpose(0, 2, 4, 3, 1)
    ).reshape(64, 128, 2048)


def _col16(v):
    return v.reshape(16, 128).T


def kernel(L0_attn_x, L0_attn_Wk, L0_attn_bk, L0_attn_Wv, L0_attn_bv,
           L0_attn_Wr, L0_attn_br, L0_attn_Wo, L0_attn_bo, L0_attn_tmk,
           L0_attn_tmv, L0_attn_tmr, L0_attn_time_decay, L0_attn_time_first,
           L0_ffn_W1, L0_ffn_W2, L0_ffn_W3):
    x = np.asarray(L0_attn_x, dtype=np.float32)
    B, T, D = x.shape
    xall = np.ascontiguousarray(x.reshape(B * T, D))

    wk_t = _tile_proj(np.asarray(L0_attn_Wk, np.float32))
    wv_t = _tile_proj(np.asarray(L0_attn_Wv, np.float32))
    wr_t = _tile_proj(np.asarray(L0_attn_Wr, np.float32))
    wo_t = _tile_proj(np.asarray(L0_attn_Wo, np.float32))
    w1_t = _tile_proj(np.asarray(L0_ffn_W1, np.float32))
    w3_t = _tile_proj(np.asarray(L0_ffn_W3, np.float32))
    w2_t = _tile_w2(np.asarray(L0_ffn_W2, np.float32))

    vv = np.concatenate([
        _col16(np.asarray(a, np.float32).reshape(-1)) for a in
        [L0_attn_tmk, L0_attn_tmv, L0_attn_tmr, L0_attn_bk, L0_attn_bv,
         L0_attn_br, L0_attn_time_decay, L0_attn_time_first, L0_attn_bo]
    ], axis=1)
    vv = np.ascontiguousarray(vv)

    H = np.array([[1.0]], np.float32)
    for _ in range(7):
        H = np.block([[H, H], [H, -H]])
    h128_a = H.astype(ml_dtypes.bfloat16)

    in_maps = []
    qper = (B * T) // NCORES
    qps = T // qper
    for c in range(NCORES):
        r0 = c * qper
        q = c % qps
        prev = np.zeros((1, D), np.float32) if q == 0 else xall[r0 - 1:r0]
        blk = np.concatenate([prev, xall[r0:r0 + qper]], axis=0)
        xT_c = np.ascontiguousarray(blk.T)
        cp = np.full(NCORES, 1e9, np.float32)
        for ci in range(NCORES):
            if (ci // qps) == (c // qps) and (ci % qps) < q:
                cp[ci] = 1024.0 * (q - (ci % qps) - 1)
        cpow_c = np.ascontiguousarray(
            np.broadcast_to(cp[None, :], (P, NCORES)).astype(np.float32))
        in_maps.append(dict(
            xT=xT_c, wk=wk_t, wv=wv_t, wr=wr_t, wo=wo_t,
            w1=w1_t, w3=w3_t, w2=w2_t, vecs=vv, cpow=cpow_c, h128=h128_a))

    nc = build()
    res = run_bass_kernel_spmd(nc, in_maps, core_ids=list(range(NCORES)))
    out = np.empty((B * T, D), np.float32)
    for c in range(NCORES):
        out[c * qper:(c + 1) * qper] = res.results[c]["out_fm"].T
    return out.reshape(B, T, D)


if __name__ == "__main__":
    import reference
    inp = {k: np.asarray(v) for k, v in reference.setup_inputs().items()}
    got = kernel(**inp)
    exp = np.asarray(reference.reference(**inp))
    num = float(((got - exp) ** 2).mean())
    den = float((exp ** 2).mean())
    print("resvar", num / den)


# version 25
# speedup vs baseline: 1.4087x; 1.4087x over previous
import contextlib

import numpy as np
import ml_dtypes
import concourse.bass as bass
import concourse.mybir as mybir
from concourse.bass_utils import run_bass_kernel_spmd

F32 = mybir.dt.float32
BF16 = mybir.dt.bfloat16
AF = mybir.ActivationFunctionType
ALU = mybir.AluOpType

P = 128
NCORES = 8
TPC = 1024
G = 16
GA = 64
S8192 = float(np.float32(1.0) / np.float32(np.sqrt(np.float32(8192.0))))

VC = {n: i * 16 for i, n in enumerate(
    ["tmk", "tmv", "tmr", "bk", "bv", "br", "td", "tf", "bo"])}


class Sem:
    def __init__(self, h):
        self.h = h
        self.n = 0

    def nxt(self, d):
        self.n += d
        return self.n


def build():
    nc = bass.Bass("TRN2", target_bir_lowering=False)

    xT = nc.dram_tensor("xT", [2048, TPC + 1], F32, kind="ExternalInput").ap()
    wk = nc.dram_tensor("wk", [G, P, 2048], F32, kind="ExternalInput").ap()
    wv = nc.dram_tensor("wv", [G, P, 2048], F32, kind="ExternalInput").ap()
    wr = nc.dram_tensor("wr", [G, P, 2048], F32, kind="ExternalInput").ap()
    wo = nc.dram_tensor("wo", [G, P, 2048], F32, kind="ExternalInput").ap()
    w1 = nc.dram_tensor("w1", [GA, P, 2048], F32, kind="ExternalInput").ap()
    w3 = nc.dram_tensor("w3", [GA, P, 2048], F32, kind="ExternalInput").ap()
    w2 = nc.dram_tensor("w2", [GA, P, 2048], F32, kind="ExternalInput").ap()
    vecs = nc.dram_tensor("vecs", [P, 144], F32, kind="ExternalInput").ap()
    cpow = nc.dram_tensor("cpow", [P, 8], F32, kind="ExternalInput").ap()
    h128 = nc.dram_tensor("h128", [P, P], BF16, kind="ExternalInput").ap()
    out_fm = nc.dram_tensor("out_fm", [2048, TPC], F32, kind="ExternalOutput").ap()

    cc_in = nc.dram_tensor("cc_in", [2, P, 16], F32)
    cc_out = nc.dram_tensor("cc_out", [NCORES, 2, P, 16], F32)

    EP = {k: [] for k in ("sync", "tensor", "vector", "scalar", "gpsimd")}

    def E(eng, fn):
        EP[eng].append(fn)

    def W(eng, sem, tok):
        EP[eng].append(lambda e, s=sem.h, t=tok: e.wait_ge(s, t))

    def DMA(eng, sem, out, in_):
        tok = sem.nxt(16)
        EP[eng].append(lambda e, o=out, i=in_, s=sem.h:
                       e.dma_start(out=o, in_=i).then_inc(s, 16))
        return tok

    def OP(eng, sem, fn):
        tok = sem.nxt(1)
        EP[eng].append(lambda e, f=fn, s=sem.h: f(e).then_inc(s, 1))
        return tok

    with contextlib.ExitStack() as ctx:
        ent = ctx.enter_context
        XST = TPC + 2
        xn = ent(nc.sbuf_tensor("xn", [P, G * XST], BF16))
        b1 = ent(nc.sbuf_tensor("b1", [P, G * TPC], BF16))
        bigb = ent(nc.sbuf_tensor("bigb", [P, 2 * G * TPC], BF16))
        ssa = ent(nc.sbuf_tensor("ssa", [P, 2 * (TPC + 1)], F32))
        smm = ent(nc.sbuf_tensor("smm", [P, 2 * (TPC + 1)], F32))
        wf = [ent(nc.sbuf_tensor(f"wf{i}", [P, 2048], F32)) for i in range(2)]
        wb = [ent(nc.sbuf_tensor(f"wb{i}", [P, 2048], BF16)) for i in range(4)]
        vecs_s = ent(nc.sbuf_tensor("vecs_s", [P, 144], F32))
        cpow_s = ent(nc.sbuf_tensor("cpow_s", [P, 8], F32))
        h128_s = ent(nc.sbuf_tensor("h128_s", [P, P], BF16))
        dcol = ent(nc.sbuf_tensor("dcol", [P, 144], F32))
        gab = ent(nc.sbuf_tensor("gab", [P, 256], F32))
        lae = ent(nc.sbuf_tensor("lae", [P, 64], F32))
        ctab = ent(nc.sbuf_tensor("ctab", [P, 128], F32))
        ones1 = ent(nc.sbuf_tensor("ones1", [P, 1], F32))
        ones1k = ent(nc.sbuf_tensor("ones1k", [P, TPC], F32))
        lamp1 = ent(nc.sbuf_tensor("lamp1", [P, TPC], F32))
        onesr = ent(nc.sbuf_tensor("onesr", [1, P], F32))

        stats_ctx = contextlib.ExitStack()
        stat_m = stats_ctx.enter_context(nc.sbuf_tensor("stat_m", [1, TPC + 1], F32))
        stat_q = stats_ctx.enter_context(nc.sbuf_tensor("stat_q", [1, TPC + 1], F32))
        stat_t = stats_ctx.enter_context(nc.sbuf_tensor("stat_t", [1, TPC + 1], F32))

        p0 = ent(nc.psum_tensor("p0", [P, 1024], F32))
        p1 = ent(nc.psum_tensor("p1", [P, 1024], F32))
        p2 = ent(nc.psum_tensor("p2", [P, 1024], F32))
        p3 = ent(nc.psum_tensor("p3", [P, 1024], F32))
        PS = [p0, p1, p2, p3]

        s_w = Sem(ent(nc.semaphore("s_w")))
        s_g = Sem(ent(nc.semaphore("s_g")))
        s_mm = Sem(ent(nc.semaphore("s_mm")))
        s_v = Sem(ent(nc.semaphore("s_v")))
        s_a = Sem(ent(nc.semaphore("s_a")))
        s_cc = Sem(ent(nc.semaphore("s_cc")))
        s_st = Sem(ent(nc.semaphore("s_st")))

        def xn_own(g):
            return xn[:, g * XST + 2:(g + 1) * XST]

        def xn_prev(g):
            return xn[:, g * XST + 1:(g + 1) * XST - 1]

        def b1t(g):
            return b1[:, g * TPC:(g + 1) * TPC]

        def bigL(g):
            return bigb[:, g * TPC:(g + 1) * TPC]

        def bigR(g):
            return bigb[:, (G + g) * TPC:(G + g + 1) * TPC]

        def vcol(name, g):
            return vecs_s[:, VC[name] + g:VC[name] + g + 1]

        d_decay = dcol[:, 0:16]
        d_lam = dcol[:, 16:32]
        d_cu = dcol[:, 32:48]
        d_ct1 = dcol[:, 48:64]
        d_ct2 = dcol[:, 64:80]
        d_om = {"tmk": dcol[:, 80:96], "tmv": dcol[:, 96:112],
                "tmr": dcol[:, 112:128]}

        class WStream:
            def __init__(self):
                self.i = 0
                self.dma_tok = {}
                self.cast_tok = {}
                self.cons_tok = {}

            def load(self, src):
                i = self.i
                self.i += 1
                if i >= 2:
                    W("sync", s_a, self.cast_tok[i - 2])
                self.dma_tok[i] = DMA("sync", s_w, wf[i % 2][:, :], src)
                return i

            def cast(self, i):
                W("scalar", s_w, self.dma_tok[i])
                if i >= 4:
                    W("scalar", s_mm, self.cons_tok[i - 4])
                self.cast_tok[i] = OP(
                    "scalar", s_a,
                    lambda e, i=i: e.copy(wb[i % 4][:, :], wf[i % 2][:, :]))
                return self.cast_tok[i]

            def consumed(self, i, tok):
                self.cons_tok[i] = tok

        ws = WStream()

        t_vecs = DMA("gpsimd", s_g, vecs_s[:, :], vecs)
        t_cpow = DMA("gpsimd", s_g, cpow_s[:, :], cpow)
        t_h128 = DMA("gpsimd", s_g, h128_s[:, :], h128)

        E("vector", lambda e: e.memset(ones1[:, :], 1.0))
        E("vector", lambda e: e.memset(ones1k[:, :], 1.0))
        t_ones = OP("vector", s_v, lambda e: e.memset(onesr[:, :], 1.0))
        W("vector", s_g, t_vecs)
        for nm in ("tmk", "tmv"):
            E("vector", lambda e, nm=nm: e.tensor_scalar(
                d_om[nm], vecs_s[:, VC[nm]:VC[nm] + 16], -1.0, 1.0,
                ALU.mult, ALU.add))
        t_om = OP("vector", s_v, lambda e: e.tensor_scalar(
            d_om["tmr"], vecs_s[:, VC["tmr"]:VC["tmr"] + 16], -1.0, 1.0,
            ALU.mult, ALU.add))

        W("scalar", s_g, t_vecs)
        E("scalar", lambda e: e.activation(
            d_ct1, vecs_s[:, VC["td"]:VC["td"] + 16], AF.Exp))
        E("scalar", lambda e: e.drain())
        E("scalar", lambda e: e.mul(d_decay, d_ct1, -1.0))
        E("scalar", lambda e: e.drain())
        E("scalar", lambda e: e.activation(d_lam, d_decay, AF.Exp))
        E("scalar", lambda e: e.drain())
        t_dvec = OP("scalar", s_a, lambda e: e.activation(
            d_cu, vecs_s[:, VC["tf"]:VC["tf"] + 16], AF.Exp))

        Xst = [ssa[:, 0:TPC + 1], ssa[:, TPC + 1:2 * (TPC + 1)]]
        Qst = [smm[:, 0:TPC + 1], smm[:, TPC + 1:2 * (TPC + 1)]]
        t_xld = {}
        t_xsq = {}
        t_mmstat = {}
        for g in range(G):
            if g >= 2:
                W("gpsimd", s_mm, t_mmstat[g - 2])
            t_xld[g] = DMA("gpsimd", s_g, Xst[g % 2][:, :],
                           xT[g * P:(g + 1) * P, :])
            W("scalar", s_g, t_xld[g])
            if g >= 2:
                W("scalar", s_mm, t_mmstat[g - 2])
            t_xsq[g] = OP("scalar", s_a, lambda e, g=g: e.activation(
                Qst[g % 2][:, :], Xst[g % 2][:, :], AF.Square))
            if g == 0:
                W("tensor", s_v, t_ones)
            W("tensor", s_a, t_xsq[g])
            st, sp = (g == 0), (g == G - 1)
            for reg, src in [
                (p0[0:1, 0:512], lambda gg: Xst[gg % 2][:, 0:512]),
                (p0[0:1, 512:1024], lambda gg: Xst[gg % 2][:, 512:1024]),
                (p1[0:1, 0:1], lambda gg: Xst[gg % 2][:, 1024:1025]),
                (p2[0:1, 0:512], lambda gg: Qst[gg % 2][:, 0:512]),
                (p2[0:1, 512:1024], lambda gg: Qst[gg % 2][:, 512:1024]),
            ]:
                E("tensor", lambda e, g=g, st=st, sp=sp, reg=reg, src=src:
                  e.matmul(reg, ones1[:, :], src(g), start=st, stop=sp,
                           skip_group_check=True))
            t_mmstat[g] = OP("tensor", s_mm, lambda e, g=g, st=st, sp=sp:
                             e.matmul(p3[0:1, 0:1], ones1[:, :],
                                      Qst[g % 2][:, 1024:1025],
                                      start=st, stop=sp, skip_group_check=True))

        W("scalar", s_mm, t_mmstat[G - 1])
        E("scalar", lambda e: e.activation(stat_m[0:1, 0:1024], p0[0:1, 0:1024],
                                           AF.Identity, bias=0.0, scale=1.0 / 2048))
        E("scalar", lambda e: e.activation(stat_m[0:1, 1024:1025], p1[0:1, 0:1],
                                           AF.Identity, bias=0.0, scale=1.0 / 2048))
        E("scalar", lambda e: e.activation(stat_q[0:1, 0:1024], p2[0:1, 0:1024],
                                           AF.Identity, bias=0.0, scale=1.0 / 2048))
        t_statev = OP("scalar", s_a, lambda e: e.activation(
            stat_q[0:1, 1024:1025], p3[0:1, 0:1], AF.Identity,
            bias=0.0, scale=1.0 / 2048))
        W("vector", s_a, t_statev)
        E("vector", lambda e: e.tensor_tensor(
            stat_t[0:1, :], stat_m[0:1, :], stat_m[0:1, :], ALU.mult))
        E("vector", lambda e: e.tensor_tensor(
            stat_t[0:1, :], stat_q[0:1, :], stat_t[0:1, :], ALU.subtract))
        t_var = OP("vector", s_v, lambda e: e.tensor_scalar_add(
            stat_t[0:1, :], stat_t[0:1, :], 1e-5))
        W("scalar", s_v, t_var)
        E("scalar", lambda e: e.activation(stat_t[0:1, :], stat_t[0:1, :], AF.Ln))
        t_rstd = OP("scalar", s_a, lambda e: e.activation(
            stat_t[0:1, :], stat_t[0:1, :], AF.Exp, bias=0.0, scale=-0.5))
        W("vector", s_a, t_rstd)
        t_mr = OP("vector", s_v, lambda e: e.tensor_tensor(
            stat_m[0:1, :], stat_m[0:1, :], stat_t[0:1, :], ALU.mult))

        W("tensor", s_v, t_mr)
        W("tensor", s_a, t_statev)
        for reg, src in [
            (p0[:, 0:512], stat_t[0:1, 0:512]),
            (p0[:, 512:1024], stat_t[0:1, 512:1024]),
            (p1[:, 0:1], stat_t[0:1, 1024:1025]),
            (p2[:, 0:512], stat_m[0:1, 0:512]),
            (p2[:, 512:1024], stat_m[0:1, 512:1024]),
        ]:
            E("tensor", lambda e, reg=reg, src=src: e.matmul(
                reg, onesr[:, :], src, start=True, stop=True,
                skip_group_check=True))
        t_mmrep = OP("tensor", s_mm, lambda e: e.matmul(
            p3[:, 0:1], onesr[:, :], stat_m[0:1, 1024:1025],
            start=True, stop=True, skip_group_check=True))
        W("scalar", s_mm, t_mmrep)
        E("scalar", lambda e: e.activation(smm[:, 0:1024], p0[:, 0:1024],
                                           AF.Identity, bias=0.0, scale=1.0))
        E("scalar", lambda e: e.activation(smm[:, 1024:1025], p1[:, 0:1],
                                           AF.Identity, bias=0.0, scale=1.0))
        E("scalar", lambda e: e.activation(smm[:, 1025:2049], p2[:, 0:1024],
                                           AF.Identity, bias=0.0, scale=1.0))
        t_rep = OP("scalar", s_a, lambda e: e.activation(
            smm[:, 2049:2050], p3[:, 0:1], AF.Identity, bias=0.0, scale=1.0))
        rstd_rep = smm[:, 0:TPC + 1]
        mr_rep = smm[:, TPC + 1:2 * (TPC + 1)]

        t_xn = {}
        for g in range(G):
            if g >= 2:
                W("gpsimd", s_v, t_xn[g - 2])
            else:
                W("gpsimd", s_mm, t_mmstat[G - 2 + g])
            tl = DMA("gpsimd", s_g, Xst[g % 2][:, :], xT[g * P:(g + 1) * P, :])
            W("vector", s_g, tl)
            if g == 0:
                W("vector", s_a, t_rep)
            E("vector", lambda e, g=g: e.tensor_tensor(
                Xst[g % 2][:, :], Xst[g % 2][:, :], rstd_rep, ALU.mult))
            t_xn[g] = OP("vector", s_v, lambda e, g=g: e.tensor_tensor(
                xn[:, g * XST + 1:(g + 1) * XST], Xst[g % 2][:, :],
                mr_rep, ALU.subtract))

        def lerp(dst_fn, tm_name, wait_tok=None):
            toks = {}
            for g in range(G):
                E("vector", lambda e, g=g: e.tensor_scalar_mul(
                    dst_fn(g), xn_own(g), vcol(tm_name, g)))
                toks[g] = OP("vector", s_v, lambda e, g=g: e.scalar_tensor_tensor(
                    dst_fn(g), xn_prev(g), d_om[tm_name][:, g:g + 1], dst_fn(g),
                    ALU.mult, ALU.add))
            return toks

        t_xk = lerp(b1t, "tmk")

        def proj_mms(i, ps, rhs_fn, tok_sem=True):
            t = None
            for dc in range(G):
                st, sp = (dc == 0), (dc == G - 1)
                for half in range(2):
                    fn = (lambda e, i=i, dc=dc, st=st, sp=sp, ps=ps, half=half:
                          e.matmul(ps[:, half * 512:(half + 1) * 512],
                                   wb[i % 4][:, dc * 128:(dc + 1) * 128],
                                   rhs_fn(dc, half), start=st, stop=sp,
                                   skip_group_check=True))
                    if dc == G - 1 and half == 1:
                        t = OP("tensor", s_mm, fn)
                    else:
                        E("tensor", fn)
            return t

        def b1_rhs(dc, half):
            return b1[:, dc * TPC + half * 512:dc * TPC + (half + 1) * 512]

        def bigR_rhs(dc, half):
            return bigb[:, (G + dc) * TPC + half * 512:
                        (G + dc) * TPC + (half + 1) * 512]

        t_kmm = {}
        t_kev = {}
        pend = {}
        pend[0] = ws.load(wk[0])
        ws.cast(pend[0])
        for ct in range(G):
            if ct + 1 < G:
                pend[ct + 1] = ws.load(wk[ct + 1])
                ws.cast(pend[ct + 1])
            i = pend[ct]
            if ct >= 1:
                W("scalar", s_mm, t_kmm[ct - 1])
                t_kev[ct - 1] = OP("scalar", s_a, lambda e, ct=ct - 1: e.activation(
                    bigL(ct), PS[ct % 4][:, :], AF.Exp,
                    bias=vcol("bk", ct), scale=1.0))
            W("tensor", s_a, ws.cast_tok[i])
            if ct == 0:
                W("tensor", s_v, t_xk[G - 1])
                W("tensor", s_a, t_rep)
            if ct >= 4:
                W("tensor", s_a, t_kev[ct - 4])
            t_kmm[ct] = proj_mms(i, PS[ct % 4], b1_rhs)
            ws.consumed(i, t_kmm[ct])
        W("scalar", s_mm, t_kmm[G - 1])
        t_kev[G - 1] = OP("scalar", s_a, lambda e, ct=G - 1: e.activation(
            bigL(ct), PS[ct % 4][:, :], AF.Exp, bias=vcol("bk", ct), scale=1.0))

        t_xv = lerp(bigR, "tmv")

        SA2 = ssa[:, 0:TPC]
        SB2 = ssa[:, TPC:2 * TPC]
        SCR = smm[:, 0:TPC]
        lam_rep = smm[:, TPC:2 * TPC]

        vtmp = [smm[:, 0:TPC], smm[:, TPC:2 * TPC]]
        t_vmm = {}
        t_vev = {}
        t_ekv = {}
        t_p1 = None
        pend = {}
        pend[0] = ws.load(wv[0])
        ws.cast(pend[0])
        for ct in range(G):
            if ct + 1 < G:
                pend[ct + 1] = ws.load(wv[ct + 1])
                ws.cast(pend[ct + 1])
            i = pend[ct]
            if ct >= 1:
                c = ct - 1
                W("scalar", s_mm, t_vmm[c])
                if c == 0:
                    W("scalar", s_v, t_xn[G - 1])
                if c >= 2:
                    W("scalar", s_v, t_ekv[c - 2])
                t_vev[c] = OP("scalar", s_a, lambda e, c=c: e.activation(
                    vtmp[c % 2][:, :], PS[c % 4][:, :], AF.Identity,
                    bias=vcol("bv", c), scale=1.0))
                W("vector", s_a, t_vev[c])
                if c == 0:
                    W("vector", s_mm, t_kmm[G - 1])
                t_ekv[c] = OP("vector", s_v, lambda e, c=c: e.tensor_tensor(
                    b1t(c), bigL(c), vtmp[c % 2][:, :], ALU.mult))
                if c == 0:
                    W("vector", s_a, t_dvec)
                E("vector", lambda e, c=c: e.tensor_scalar_mul(
                    lamp1[:, :], ones1k[:, :], d_lam[:, c:c + 1]))
                E("vector", lambda e, c=c: e.tensor_tensor_scan(
                    SA2, lamp1[:, :], b1t(c), 0.0, ALU.mult, ALU.add))
                E("vector", lambda e, c=c: e.tensor_tensor_scan(
                    SB2, lamp1[:, :], bigL(c), 0.0, ALU.mult, ALU.add))
                E("vector", lambda e: e.drain())
                E("vector", lambda e, c=c: e.tensor_copy(
                    lae[:, c:c + 1], SA2[:, TPC - 1:TPC]))
                t_p1 = OP("vector", s_v, lambda e, c=c: e.tensor_copy(
                    lae[:, 16 + c:17 + c], SB2[:, TPC - 1:TPC]))
            W("tensor", s_a, ws.cast_tok[i])
            if ct == 0:
                W("tensor", s_v, t_xv[G - 1])
                W("tensor", s_a, t_kev[G - 1])
            if ct >= 4:
                W("tensor", s_a, t_vev[ct - 4])
            t_vmm[ct] = proj_mms(i, PS[ct % 4], bigR_rhs)
            ws.consumed(i, t_vmm[ct])
        c = G - 1
        W("scalar", s_mm, t_vmm[c])
        W("scalar", s_v, t_ekv[c - 2])
        t_vev[c] = OP("scalar", s_a, lambda e, c=c: e.activation(
            vtmp[c % 2][:, :], PS[c % 4][:, :], AF.Identity,
            bias=vcol("bv", c), scale=1.0))
        W("vector", s_a, t_vev[c])
        t_ekv[c] = OP("vector", s_v, lambda e, c=c: e.tensor_tensor(
            b1t(c), bigL(c), vtmp[c % 2][:, :], ALU.mult))
        E("vector", lambda e, c=c: e.tensor_scalar_mul(
            lamp1[:, :], ones1k[:, :], d_lam[:, c:c + 1]))
        E("vector", lambda e, c=c: e.tensor_tensor_scan(
            SA2, lamp1[:, :], b1t(c), 0.0, ALU.mult, ALU.add))
        E("vector", lambda e, c=c: e.tensor_tensor_scan(
            SB2, lamp1[:, :], bigL(c), 0.0, ALU.mult, ALU.add))
        E("vector", lambda e: e.drain())
        E("vector", lambda e, c=c: e.tensor_copy(
            lae[:, c:c + 1], SA2[:, TPC - 1:TPC]))
        t_p1 = OP("vector", s_v, lambda e, c=c: e.tensor_copy(
            lae[:, 16 + c:17 + c], SB2[:, TPC - 1:TPC]))

        W("gpsimd", s_v, t_p1)
        DMA("gpsimd", s_g, cc_in[0], lae[:, 0:16])
        t_ccin = DMA("gpsimd", s_g, cc_in[1], lae[:, 16:32])
        W("gpsimd", s_g, t_ccin)
        t_ccdone = OP("gpsimd", s_cc, lambda e: e.collective_compute(
            "AllGather", ALU.bypass, replica_groups=[list(range(NCORES))],
            ins=[cc_in.ap().opt()], outs=[cc_out.ap().opt()]))
        W("gpsimd", s_cc, t_ccdone)
        t_gab = None
        for c in range(NCORES):
            DMA("gpsimd", s_g, gab[:, c * 16:(c + 1) * 16], cc_out[c, 0])
            t_gab = DMA("gpsimd", s_g, gab[:, 128 + c * 16:128 + (c + 1) * 16],
                        cc_out[c, 1])

        W("vector", s_mm, t_vmm[G - 1])
        t_xr = lerp(bigR, "tmr")

        W("vector", s_g, t_gab)
        W("vector", s_g, t_cpow)
        for c in range(NCORES - 1):
            E("vector", lambda e, c=c: e.tensor_scalar_mul(
                ctab[:, c * 16:(c + 1) * 16], d_decay, cpow_s[:, c:c + 1]))
        t_ctab = OP("vector", s_v, lambda e, c=NCORES - 1: e.tensor_scalar_mul(
            ctab[:, c * 16:(c + 1) * 16], d_decay, cpow_s[:, c:c + 1]))
        W("scalar", s_v, t_ctab)
        t_cexp = OP("scalar", s_a,
                    lambda e: e.activation(ctab[:, :], ctab[:, :], AF.Exp))
        W("vector", s_a, t_cexp)
        E("vector", lambda e: e.tensor_tensor(
            gab[:, 0:128], ctab[:, :], gab[:, 0:128], ALU.mult))
        E("vector", lambda e: e.tensor_tensor(
            gab[:, 128:256], ctab[:, :], gab[:, 128:256], ALU.mult))
        E("vector", lambda e: e.drain())
        E("vector", lambda e: e.tensor_reduce(
            lae[:, 32:48],
            gab[:, 0:128].rearrange("p (c g) -> p g c", c=8, g=16),
            mybir.AxisListType.X, ALU.add))
        E("vector", lambda e: e.drain())
        t_astart = OP("vector", s_v, lambda e: e.tensor_reduce(
            lae[:, 48:64],
            gab[:, 128:256].rearrange("p (c g) -> p g c", c=8, g=16),
            mybir.AxisListType.X, ALU.add))


        stats_ctx.close()
        sgb = ent(nc.sbuf_tensor("sgb", [P, 2048], BF16))

        t_rmm = {}
        t_nd = {}
        t_recip = {}
        t_wkv = {}
        t_sig = {}
        t_rwkv = {}
        pend = {}
        pend[0] = ws.load(wr[0])
        ws.cast(pend[0])
        for g in range(G):
            if g + 1 < G:
                pend[g + 1] = ws.load(wr[g + 1])
                ws.cast(pend[g + 1])
            i = pend[g]
            W("tensor", s_a, ws.cast_tok[i])
            if g == 0:
                W("tensor", s_v, t_xr[G - 1])
                W("tensor", s_a, t_vev[G - 1])
            if g >= 4:
                W("tensor", s_a, t_sig[g - 4])
            t_rmm[g] = proj_mms(i, PS[g % 4], bigR_rhs)
            ws.consumed(i, t_rmm[g])

            if g == 0:
                W("vector", s_v, t_astart)
            E("vector", lambda e, g=g: e.tensor_scalar_mul(
                lam_rep, ones1k[:, :], d_lam[:, g:g + 1]))
            E("vector", lambda e, g=g: e.tensor_tensor_scan(
                SCR, lam_rep, b1t(g), lae[:, 32 + g:33 + g],
                ALU.mult, ALU.add))
            E("vector", lambda e, g=g: e.scalar_tensor_tensor(
                SA2[:, 1:TPC], b1t(g)[:, 1:TPC], d_cu[:, g:g + 1],
                SCR[:, 0:TPC - 1], ALU.mult, ALU.add))
            E("vector", lambda e, g=g: e.scalar_tensor_tensor(
                SA2[:, 0:1], b1t(g)[:, 0:1], d_cu[:, g:g + 1],
                lae[:, 32 + g:33 + g], ALU.mult, ALU.add))
            E("vector", lambda e, g=g: e.tensor_tensor_scan(
                SCR, lam_rep, bigL(g), lae[:, 48 + g:49 + g],
                ALU.mult, ALU.add))
            E("vector", lambda e, g=g: e.scalar_tensor_tensor(
                SB2[:, 1:TPC], bigL(g)[:, 1:TPC], d_cu[:, g:g + 1],
                SCR[:, 0:TPC - 1], ALU.mult, ALU.add))
            t_nd[g] = OP("vector", s_v, lambda e, g=g: e.scalar_tensor_tensor(
                SB2[:, 0:1], bigL(g)[:, 0:1], d_cu[:, g:g + 1],
                lae[:, 48 + g:49 + g], ALU.mult, ALU.add))
            W("scalar", s_v, t_nd[g])
            E("scalar", lambda e: e.activation(SB2, SB2, AF.Ln))
            t_recip[g] = OP("scalar", s_a, lambda e: e.activation(
                SB2, SB2, AF.Exp, bias=0.0, scale=-1.0))
            W("vector", s_a, t_recip[g])
            t_wkv[g] = OP("vector", s_v, lambda e, g=g: e.tensor_tensor(
                bigL(g), SA2, SB2, ALU.mult))
            W("scalar", s_mm, t_rmm[g])
            if g >= 2:
                W("scalar", s_v, t_rwkv[g - 2])
            t_sig[g] = OP("scalar", s_a, lambda e, g=g: e.activation(
                sgb[:, (g % 2) * 1024:(g % 2 + 1) * 1024], PS[g % 4][:, :],
                AF.Sigmoid, bias=vcol("br", g), scale=1.0))
            W("vector", s_a, t_sig[g])
            t_rwkv[g] = OP("vector", s_v, lambda e, g=g: e.tensor_tensor(
                bigL(g), sgb[:, (g % 2) * 1024:(g % 2 + 1) * 1024],
                bigL(g), ALU.mult))

        t_womm = {}
        t_woev = {}

        def wo_evac(dt):
            return lambda e, dt=dt: e.activation(
                b1t(dt), PS[dt % 4][:, :], AF.Identity,
                bias=vcol("bo", dt), scale=1.0)

        pend = {}
        pend[0] = ws.load(wo[0])
        ws.cast(pend[0])
        for dt in range(G):
            if dt + 1 < G:
                pend[dt + 1] = ws.load(wo[dt + 1])
                ws.cast(pend[dt + 1])
            i = pend[dt]
            if dt >= 1:
                W("scalar", s_mm, t_womm[dt - 1])
                if dt == 1:
                    W("scalar", s_v, t_nd[G - 1])
                t_woev[dt - 1] = OP("scalar", s_a, wo_evac(dt - 1))
            W("tensor", s_a, ws.cast_tok[i])
            if dt == 0:
                W("tensor", s_v, t_rwkv[G - 1])
                W("tensor", s_a, t_sig[G - 1])
            if dt >= 4:
                W("tensor", s_a, t_woev[dt - 4])
            t = None
            for cc in range(G):
                st, sp = (cc == 0), (cc == G - 1)
                for half in range(2):
                    fn = (lambda e, i=i, cc=cc, st=st, sp=sp, dt=dt, half=half:
                          e.matmul(PS[dt % 4][:, half * 512:(half + 1) * 512],
                                   wb[i % 4][:, cc * 128:(cc + 1) * 128],
                                   bigb[:, cc * TPC + half * 512:
                                        cc * TPC + (half + 1) * 512],
                                   start=st, stop=sp, skip_group_check=True))
                    if cc == G - 1 and half == 1:
                        t = OP("tensor", s_mm, fn)
                    else:
                        E("tensor", fn)
            t_womm[dt] = t
            ws.consumed(i, t)
        W("scalar", s_mm, t_womm[G - 1])
        t_woev[G - 1] = OP("scalar", s_a, wo_evac(G - 1))

        t_st_l = {}
        t_oev = {}
        t_w2mm_last = None
        t_hid_prev = None

        def bf_stage_pieces(h, lo_tile, n_tiles):
            nb = n_tiles // (2 * h)
            seg = bigb[:, lo_tile * 512:(lo_tile + n_tiles) * 512]
            ap5 = seg.rearrange("p (b two i t) -> p b two i t",
                                b=nb, two=2, i=h, t=512)
            out = []
            if h == 1:
                step = 2 if nb >= 2 else 1
                for b0 in range(0, nb, step):
                    bs = min(step, nb - b0)
                    out.append((ap5[:, b0:b0 + bs, 0], ap5[:, b0:b0 + bs, 1],
                                (bs, 1, 512)))
            else:
                ic = 2 if h >= 2 else 1
                for b in range(nb):
                    for i0 in range(0, h, ic):
                        out.append((ap5[:, b:b + 1, 0, i0:i0 + ic],
                                    ap5[:, b:b + 1, 1, i0:i0 + ic],
                                    (1, ic, 512)))
            return out

        bf_state = {"j": 0, "cp": {}, "last_cp": 0}

        def bf_emit(pieces_list):
            for (A, B, shp) in pieces_list:
                j = bf_state["j"]
                bf_state["j"] = j + 1
                half = sgb[:, (j % 2) * 1024:(j % 2 + 1) * 1024]
                halfr = half.rearrange("p (b i t) -> p b i t",
                                       b=shp[0], i=shp[1], t=shp[2])
                if j >= 2:
                    W("vector", s_a, bf_state["cp"][j - 2])
                E("vector", lambda e, A=A, B=B, hr=halfr: e.tensor_tensor(
                    hr, A, B, ALU.subtract))
                ta = OP("vector", s_v, lambda e, A=A, B=B: e.tensor_tensor(
                    A, A, B, ALU.add))
                W("scalar", s_v, ta)
                tcp = OP("scalar", s_a, lambda e, B=B, hr=halfr: e.copy(B, hr))
                bf_state["cp"][j] = tcp
                bf_state["last_cp"] = tcp

        for tc in range(2):
            bf_queue = []
            t_ffmm = {}
            t_hid = {}
            t_silu = {}
            pendf = {}
            a0p = (ws.load(w1[0]), ws.load(w3[0]))
            ws.cast(a0p[0])
            ws.cast(a0p[1])
            pendf[0] = a0p
            for a in range(GA):
                if a + 1 < GA:
                    n1 = ws.load(w1[a + 1])
                    ws.cast(n1)
                    n3 = ws.load(w3[a + 1])
                    ws.cast(n3)
                    pendf[a + 1] = (n1, n3)
                i1, i3 = pendf[a]
                W("tensor", s_a, ws.cast_tok[i3])
                W("tensor", s_a, ws.cast_tok[i1])
                if a == 0 and tc == 0:
                    W("tensor", s_a, t_woev[G - 1])
                if a >= 4:
                    W("tensor", s_v, t_hid[a - 4])
                elif tc == 1:
                    W("tensor", s_v, t_hid_prev[GA - 4 + a])
                ps = PS[a % 4]
                t = None
                for dc in range(G):
                    st, sp = (dc == 0), (dc == G - 1)
                    rhs = xn[:, dc * XST + 2 + tc * 512:
                             dc * XST + 2 + tc * 512 + 512]
                    E("tensor", lambda e, i=i1, st=st, sp=sp, ps=ps, rhs=rhs, dc=dc:
                      e.matmul(ps[:, 0:512], wb[i % 4][:, dc * 128:(dc + 1) * 128],
                               rhs, start=st, stop=sp, skip_group_check=True))
                    fn = (lambda e, i=i3, st=st, sp=sp, ps=ps, rhs=rhs, dc=dc:
                          e.matmul(ps[:, 512:1024],
                                   wb[i % 4][:, dc * 128:(dc + 1) * 128],
                                   rhs, start=st, stop=sp, skip_group_check=True))
                    if dc == G - 1:
                        t = OP("tensor", s_mm, fn)
                    else:
                        E("tensor", fn)
                t_ffmm[a] = t
                ws.consumed(i1, t)
                ws.consumed(i3, t)
                W("scalar", s_mm, t_ffmm[a])
                if tc == 0 and a == 0:
                    W("scalar", s_v, t_wkv[G - 1])
                if a >= 2:
                    W("scalar", s_v, t_hid[a - 2])
                t_silu[a] = OP("scalar", s_a, lambda e, a=a, ps=ps: e.activation(
                    smm[:, (a % 2) * 512:(a % 2 + 1) * 512], ps[:, 0:512], AF.Silu))
                W("vector", s_a, t_silu[a])
                if tc == 0 and a == 0:
                    W("vector", s_v, t_nd[G - 1])
                    W("vector", s_mm, t_womm[G - 1])
                if tc == 1 and a == 0:
                    W("vector", s_mm, t_w2mm_last)
                t_hid[a] = OP("vector", s_v, lambda e, a=a, ps=ps: e.tensor_tensor(
                    bigb[:, a * 512:(a + 1) * 512],
                    smm[:, (a % 2) * 512:(a % 2 + 1) * 512],
                    ps[:, 512:1024], ALU.mult))
                if a % 16 == 15:
                    gb = a // 16
                    for h in (1, 2, 4, 8):
                        bf_queue.extend(bf_stage_pieces(h, 16 * gb, 16))
                    if a % 32 == 31:
                        bf_queue.extend(bf_stage_pieces(16, 32 * (a // 32), 32))
                for _ in range(2):
                    if bf_queue:
                        bf_emit([bf_queue.pop(0)])
            bf_queue.extend(bf_stage_pieces(32, 0, 64))
            while bf_queue:
                bf_emit([bf_queue.pop(0)])
            t_hid_prev = t_hid
            t_bfly = bf_state["last_cp"]

            t_hmm = {}
            t_zev = {}
            for a in range(GA):
                if a == 0:
                    W("tensor", s_a, t_bfly)
                    W("tensor", s_g, t_h128)
                if a >= 2:
                    W("tensor", s_a, t_zev[a - 2])
                reg = p2[:, (a % 2) * 512:(a % 2 + 1) * 512]
                t_hmm[a] = OP("tensor", s_mm, lambda e, a=a, reg=reg: e.matmul(
                    reg, h128_s[:, :], bigb[:, a * 512:(a + 1) * 512],
                    start=True, stop=True, skip_group_check=True))
                W("scalar", s_mm, t_hmm[a])
                t_zev[a] = OP("scalar", s_a, lambda e, a=a, reg=reg: e.activation(
                    bigb[:, a * 512:(a + 1) * 512], reg, AF.Copy,
                    bias=0.0, scale=S8192))

            for dt in range(G):
                xsl = smm[:, 1024 + (dt % 2) * 512:1024 + (dt % 2 + 1) * 512]
                prev = None
                if dt >= 2:
                    prev = (tc, dt - 2)
                elif tc == 1:
                    prev = (0, G - 2 + dt)
                if prev is not None:
                    W("gpsimd", s_v, t_oev[prev])
                elif tc == 0:
                    W("gpsimd", s_v, t_nd[G - 1])
                t_xsl = DMA("gpsimd", s_g, xsl,
                            xT[dt * P:(dt + 1) * P,
                               1 + tc * 512:1 + tc * 512 + 512])
                t_w2mm = None
                for blk in range(4):
                    i = ws.load(w2[dt * 4 + blk])
                    ws.cast(i)
                    W("tensor", s_a, ws.cast_tok[i])
                    if blk == 0:
                        if dt == 0:
                            W("tensor", s_a, t_zev[GA - 1])
                        if prev is not None:
                            W("tensor", s_v, t_oev[prev])
                    reg = p3[:, (dt % 2) * 512:(dt % 2 + 1) * 512]
                    tb = None
                    for b16 in range(16):
                        b = blk * 16 + b16
                        st, sp = (b == 0), (b == GA - 1)
                        fn = (lambda e, i=i, b=b, b16=b16, st=st, sp=sp, reg=reg:
                              e.matmul(reg, wb[i % 4][:, b16 * 128:(b16 + 1) * 128],
                                       bigb[:, b * 512:(b + 1) * 512],
                                       start=st, stop=sp, skip_group_check=True))
                        if b16 == 15:
                            tb = OP("tensor", s_mm, fn)
                        else:
                            E("tensor", fn)
                    ws.consumed(i, tb)
                    if blk == 3:
                        t_w2mm = tb
                t_w2mm_last = t_w2mm
                evst = ssa[:, (dt % 2) * 512:(dt % 2 + 1) * 512]
                W("vector", s_mm, t_w2mm)
                W("vector", s_g, t_xsl)
                if prev is not None:
                    W("vector", s_st, t_st_l[prev])
                E("vector", lambda e, dt=dt, tc=tc, evst=evst: e.tensor_tensor(
                    evst, p3[:, (dt % 2) * 512:(dt % 2 + 1) * 512],
                    b1[:, dt * TPC + tc * 512:dt * TPC + tc * 512 + 512],
                    ALU.add))
                t_oev[(tc, dt)] = OP("vector", s_v, lambda e, evst=evst, xsl=xsl:
                                     e.tensor_tensor(evst, evst, xsl, ALU.add))
                W("gpsimd", s_v, t_oev[(tc, dt)])
                t_st_l[(tc, dt)] = DMA(
                    "gpsimd", s_st,
                    out_fm[dt * P:(dt + 1) * P, tc * 512:(tc + 1) * 512], evst)

        W("gpsimd", s_st, s_st.n)
        W("sync", s_st, s_st.n)

        block = ent(nc.Block())

        @block.sync
        def _(e):
            for f in EP["sync"]:
                f(e)

        @block.tensor
        def _(e):
            for f in EP["tensor"]:
                f(e)

        @block.vector
        def _(e):
            for f in EP["vector"]:
                f(e)

        @block.scalar
        def _(e):
            for f in EP["scalar"]:
                f(e)

        @block.gpsimd
        def _(e):
            for f in EP["gpsimd"]:
                f(e)

    return nc


def _tile_proj(wm):
    o, h = wm.shape
    return np.ascontiguousarray(
        wm.reshape(o // 128, 128, h // 128, 128).transpose(0, 3, 2, 1)
    ).reshape(o // 128, 128, h)


def _tile_w2(wm):
    return np.ascontiguousarray(
        wm.reshape(16, 128, 4, 16, 128).transpose(0, 2, 4, 3, 1)
    ).reshape(64, 128, 2048)


def _col16(v):
    return v.reshape(16, 128).T


def kernel(L0_attn_x, L0_attn_Wk, L0_attn_bk, L0_attn_Wv, L0_attn_bv,
           L0_attn_Wr, L0_attn_br, L0_attn_Wo, L0_attn_bo, L0_attn_tmk,
           L0_attn_tmv, L0_attn_tmr, L0_attn_time_decay, L0_attn_time_first,
           L0_ffn_W1, L0_ffn_W2, L0_ffn_W3):
    x = np.asarray(L0_attn_x, dtype=np.float32)
    B, T, D = x.shape
    xall = np.ascontiguousarray(x.reshape(B * T, D))

    wk_t = _tile_proj(np.asarray(L0_attn_Wk, np.float32))
    wv_t = _tile_proj(np.asarray(L0_attn_Wv, np.float32))
    wr_t = _tile_proj(np.asarray(L0_attn_Wr, np.float32))
    wo_t = _tile_proj(np.asarray(L0_attn_Wo, np.float32))
    w1_t = _tile_proj(np.asarray(L0_ffn_W1, np.float32))
    w3_t = _tile_proj(np.asarray(L0_ffn_W3, np.float32))
    w2_t = _tile_w2(np.asarray(L0_ffn_W2, np.float32))

    vv = np.concatenate([
        _col16(np.asarray(a, np.float32).reshape(-1)) for a in
        [L0_attn_tmk, L0_attn_tmv, L0_attn_tmr, L0_attn_bk, L0_attn_bv,
         L0_attn_br, L0_attn_time_decay, L0_attn_time_first, L0_attn_bo]
    ], axis=1)
    vv = np.ascontiguousarray(vv)

    H = np.array([[1.0]], np.float32)
    for _ in range(7):
        H = np.block([[H, H], [H, -H]])
    h128_a = H.astype(ml_dtypes.bfloat16)

    in_maps = []
    qper = (B * T) // NCORES
    qps = T // qper
    for c in range(NCORES):
        r0 = c * qper
        q = c % qps
        prev = np.zeros((1, D), np.float32) if q == 0 else xall[r0 - 1:r0]
        blk = np.concatenate([prev, xall[r0:r0 + qper]], axis=0)
        xT_c = np.ascontiguousarray(blk.T)
        cp = np.full(NCORES, 1e9, np.float32)
        for ci in range(NCORES):
            if (ci // qps) == (c // qps) and (ci % qps) < q:
                cp[ci] = 1024.0 * (q - (ci % qps) - 1)
        cpow_c = np.ascontiguousarray(
            np.broadcast_to(cp[None, :], (P, NCORES)).astype(np.float32))
        in_maps.append(dict(
            xT=xT_c, wk=wk_t, wv=wv_t, wr=wr_t, wo=wo_t,
            w1=w1_t, w3=w3_t, w2=w2_t, vecs=vv, cpow=cpow_c, h128=h128_a))

    nc = build()
    res = run_bass_kernel_spmd(nc, in_maps, core_ids=list(range(NCORES)))
    out = np.empty((B * T, D), np.float32)
    for c in range(NCORES):
        out[c * qper:(c + 1) * qper] = res.results[c]["out_fm"].T
    return out.reshape(B, T, D)


if __name__ == "__main__":
    import reference
    inp = {k: np.asarray(v) for k, v in reference.setup_inputs().items()}
    got = kernel(**inp)
    exp = np.asarray(reference.reference(**inp))
    num = float(((got - exp) ** 2).mean())
    den = float((exp ** 2).mean())
    print("resvar", num / den)


# version 34
# speedup vs baseline: 1.4176x; 1.0063x over previous
import contextlib

import numpy as np
import ml_dtypes
import concourse.bass as bass
import concourse.mybir as mybir
from concourse.bass_utils import run_bass_kernel_spmd

F32 = mybir.dt.float32
BF16 = mybir.dt.bfloat16
AF = mybir.ActivationFunctionType
ALU = mybir.AluOpType

P = 128
NCORES = 8
TPC = 1024
G = 16
GA = 64
S8192 = float(np.float32(1.0) / np.float32(np.sqrt(np.float32(8192.0))))

VC = {n: i * 16 for i, n in enumerate(
    ["tmk", "tmv", "tmr", "bk", "bv", "br", "td", "tf", "bo"])}


class Sem:
    def __init__(self, h):
        self.h = h
        self.n = 0

    def nxt(self, d):
        self.n += d
        return self.n


def build():
    nc = bass.Bass("TRN2", target_bir_lowering=False)

    xT = nc.dram_tensor("xT", [2048, TPC + 1], F32, kind="ExternalInput").ap()
    wk = nc.dram_tensor("wk", [G, P, 2048], F32, kind="ExternalInput").ap()
    wv = nc.dram_tensor("wv", [G, P, 2048], F32, kind="ExternalInput").ap()
    wr = nc.dram_tensor("wr", [G, P, 2048], F32, kind="ExternalInput").ap()
    wo = nc.dram_tensor("wo", [G, P, 2048], F32, kind="ExternalInput").ap()
    w1 = nc.dram_tensor("w1", [GA, P, 2048], F32, kind="ExternalInput").ap()
    w3 = nc.dram_tensor("w3", [GA, P, 2048], F32, kind="ExternalInput").ap()
    w2 = nc.dram_tensor("w2", [GA, P, 2048], F32, kind="ExternalInput").ap()
    vecs = nc.dram_tensor("vecs", [P, 144], F32, kind="ExternalInput").ap()
    cpow = nc.dram_tensor("cpow", [P, 8], F32, kind="ExternalInput").ap()
    h128 = nc.dram_tensor("h128", [P, P], BF16, kind="ExternalInput").ap()
    out_fm = nc.dram_tensor("out_fm", [2048, TPC], F32, kind="ExternalOutput").ap()

    cc_in = nc.dram_tensor("cc_in", [2, P, 16], F32)
    cc_out = nc.dram_tensor("cc_out", [NCORES, 2, P, 16], F32)

    EP = {k: [] for k in ("sync", "tensor", "vector", "scalar", "gpsimd")}

    def E(eng, fn):
        EP[eng].append(fn)

    def W(eng, sem, tok):
        EP[eng].append(lambda e, s=sem.h, t=tok: e.wait_ge(s, t))

    def DMA(eng, sem, out, in_):
        tok = sem.nxt(16)
        EP[eng].append(lambda e, o=out, i=in_, s=sem.h:
                       e.dma_start(out=o, in_=i).then_inc(s, 16))
        return tok

    def OP(eng, sem, fn):
        tok = sem.nxt(1)
        EP[eng].append(lambda e, f=fn, s=sem.h: f(e).then_inc(s, 1))
        return tok

    with contextlib.ExitStack() as ctx:
        ent = ctx.enter_context
        XST = TPC + 2
        xn = ent(nc.sbuf_tensor("xn", [P, G * XST], BF16))
        b1 = ent(nc.sbuf_tensor("b1", [P, G * TPC], BF16))
        bigb = ent(nc.sbuf_tensor("bigb", [P, 2 * G * TPC], BF16))
        ssa = ent(nc.sbuf_tensor("ssa", [P, 2 * (TPC + 1)], F32))
        smm = ent(nc.sbuf_tensor("smm", [P, 2 * (TPC + 1)], F32))
        wf = [ent(nc.sbuf_tensor(f"wf{i}", [P, 2048], F32)) for i in range(2)]
        wb = [ent(nc.sbuf_tensor(f"wb{i}", [P, 2048], BF16)) for i in range(4)]
        vecs_s = ent(nc.sbuf_tensor("vecs_s", [P, 144], F32))
        cpow_s = ent(nc.sbuf_tensor("cpow_s", [P, 8], F32))
        h128_s = ent(nc.sbuf_tensor("h128_s", [P, P], BF16))
        dcol = ent(nc.sbuf_tensor("dcol", [P, 144], F32))
        gab = ent(nc.sbuf_tensor("gab", [P, 256], F32))
        lae = ent(nc.sbuf_tensor("lae", [P, 64], F32))
        ctab = ent(nc.sbuf_tensor("ctab", [P, 128], F32))
        ones1 = ent(nc.sbuf_tensor("ones1", [P, 1], F32))
        ones1k = ent(nc.sbuf_tensor("ones1k", [P, TPC], F32))
        lamp1 = ent(nc.sbuf_tensor("lamp1", [P, TPC], F32))
        onesr = ent(nc.sbuf_tensor("onesr", [1, P], F32))

        stats_ctx = contextlib.ExitStack()
        stat_m = stats_ctx.enter_context(nc.sbuf_tensor("stat_m", [1, TPC + 1], F32))
        stat_q = stats_ctx.enter_context(nc.sbuf_tensor("stat_q", [1, TPC + 1], F32))
        stat_t = stats_ctx.enter_context(nc.sbuf_tensor("stat_t", [1, TPC + 1], F32))

        p0 = ent(nc.psum_tensor("p0", [P, 1024], F32))
        p1 = ent(nc.psum_tensor("p1", [P, 1024], F32))
        p2 = ent(nc.psum_tensor("p2", [P, 1024], F32))
        p3 = ent(nc.psum_tensor("p3", [P, 1024], F32))
        PS = [p0, p1, p2, p3]

        s_w = Sem(ent(nc.semaphore("s_w")))
        s_g = Sem(ent(nc.semaphore("s_g")))
        s_mm = Sem(ent(nc.semaphore("s_mm")))
        s_v = Sem(ent(nc.semaphore("s_v")))
        s_a = Sem(ent(nc.semaphore("s_a")))
        s_cc = Sem(ent(nc.semaphore("s_cc")))
        s_st = Sem(ent(nc.semaphore("s_st")))
        s_gp = Sem(ent(nc.semaphore("s_gp")))

        def xn_own(g):
            return xn[:, g * XST + 2:(g + 1) * XST]

        def xn_prev(g):
            return xn[:, g * XST + 1:(g + 1) * XST - 1]

        def b1t(g):
            return b1[:, g * TPC:(g + 1) * TPC]

        def bigL(g):
            return bigb[:, g * TPC:(g + 1) * TPC]

        def bigR(g):
            return bigb[:, (G + g) * TPC:(G + g + 1) * TPC]

        def vcol(name, g):
            return vecs_s[:, VC[name] + g:VC[name] + g + 1]

        d_decay = dcol[:, 0:16]
        d_lam = dcol[:, 16:32]
        d_cu = dcol[:, 32:48]
        d_ct1 = dcol[:, 48:64]
        d_ct2 = dcol[:, 64:80]
        d_om = {"tmk": dcol[:, 80:96], "tmv": dcol[:, 96:112],
                "tmr": dcol[:, 112:128]}

        class WStream:
            def __init__(self):
                self.i = 0
                self.dma_tok = {}
                self.cast_tok = {}
                self.cons_tok = {}

            def load(self, src):
                i = self.i
                self.i += 1
                if i >= 2:
                    W("sync", s_a, self.cast_tok[i - 2])
                self.dma_tok[i] = DMA("sync", s_w, wf[i % 2][:, :], src)
                return i

            def cast(self, i):
                W("scalar", s_w, self.dma_tok[i])
                if i >= 4:
                    W("scalar", s_mm, self.cons_tok[i - 4])
                self.cast_tok[i] = OP(
                    "scalar", s_a,
                    lambda e, i=i: e.copy(wb[i % 4][:, :], wf[i % 2][:, :]))
                return self.cast_tok[i]

            def consumed(self, i, tok):
                self.cons_tok[i] = tok

        ws = WStream()

        t_vecs = DMA("gpsimd", s_g, vecs_s[:, :], vecs)
        t_cpow = DMA("gpsimd", s_g, cpow_s[:, :], cpow)
        t_h128 = DMA("gpsimd", s_g, h128_s[:, :], h128)

        E("vector", lambda e: e.memset(ones1[:, :], 1.0))
        E("vector", lambda e: e.memset(ones1k[:, :], 1.0))
        t_ones = OP("vector", s_v, lambda e: e.memset(onesr[:, :], 1.0))
        W("vector", s_g, t_vecs)
        for nm in ("tmk", "tmv"):
            E("vector", lambda e, nm=nm: e.tensor_scalar(
                d_om[nm], vecs_s[:, VC[nm]:VC[nm] + 16], -1.0, 1.0,
                ALU.mult, ALU.add))
        t_om = OP("vector", s_v, lambda e: e.tensor_scalar(
            d_om["tmr"], vecs_s[:, VC["tmr"]:VC["tmr"] + 16], -1.0, 1.0,
            ALU.mult, ALU.add))

        W("scalar", s_g, t_vecs)
        E("scalar", lambda e: e.activation(
            d_ct1, vecs_s[:, VC["td"]:VC["td"] + 16], AF.Exp))
        E("scalar", lambda e: e.drain())
        E("scalar", lambda e: e.mul(d_decay, d_ct1, -1.0))
        E("scalar", lambda e: e.drain())
        E("scalar", lambda e: e.activation(d_lam, d_decay, AF.Exp))
        E("scalar", lambda e: e.drain())
        t_dvec = OP("scalar", s_a, lambda e: e.activation(
            d_cu, vecs_s[:, VC["tf"]:VC["tf"] + 16], AF.Exp))

        Xst = [ssa[:, 0:TPC + 1], ssa[:, TPC + 1:2 * (TPC + 1)]]
        Qst = [smm[:, 0:TPC + 1], smm[:, TPC + 1:2 * (TPC + 1)]]
        t_xld = {}
        t_xsq = {}
        t_mmstat = {}
        for g in range(G):
            if g >= 2:
                W("gpsimd", s_mm, t_mmstat[g - 2])
            t_xld[g] = DMA("gpsimd", s_g, Xst[g % 2][:, :],
                           xT[g * P:(g + 1) * P, :])
            W("scalar", s_g, t_xld[g])
            if g >= 2:
                W("scalar", s_mm, t_mmstat[g - 2])
            t_xsq[g] = OP("scalar", s_a, lambda e, g=g: e.activation(
                Qst[g % 2][:, :], Xst[g % 2][:, :], AF.Square))
            if g == 0:
                W("tensor", s_v, t_ones)
            W("tensor", s_a, t_xsq[g])
            st, sp = (g == 0), (g == G - 1)
            for reg, src in [
                (p0[0:1, 0:512], lambda gg: Xst[gg % 2][:, 0:512]),
                (p0[0:1, 512:1024], lambda gg: Xst[gg % 2][:, 512:1024]),
                (p1[0:1, 0:1], lambda gg: Xst[gg % 2][:, 1024:1025]),
                (p2[0:1, 0:512], lambda gg: Qst[gg % 2][:, 0:512]),
                (p2[0:1, 512:1024], lambda gg: Qst[gg % 2][:, 512:1024]),
            ]:
                E("tensor", lambda e, g=g, st=st, sp=sp, reg=reg, src=src:
                  e.matmul(reg, ones1[:, :], src(g), start=st, stop=sp,
                           skip_group_check=True))
            t_mmstat[g] = OP("tensor", s_mm, lambda e, g=g, st=st, sp=sp:
                             e.matmul(p3[0:1, 0:1], ones1[:, :],
                                      Qst[g % 2][:, 1024:1025],
                                      start=st, stop=sp, skip_group_check=True))

        W("scalar", s_mm, t_mmstat[G - 1])
        E("scalar", lambda e: e.activation(stat_m[0:1, 0:1024], p0[0:1, 0:1024],
                                           AF.Identity, bias=0.0, scale=1.0 / 2048))
        E("scalar", lambda e: e.activation(stat_m[0:1, 1024:1025], p1[0:1, 0:1],
                                           AF.Identity, bias=0.0, scale=1.0 / 2048))
        E("scalar", lambda e: e.activation(stat_q[0:1, 0:1024], p2[0:1, 0:1024],
                                           AF.Identity, bias=0.0, scale=1.0 / 2048))
        t_statev = OP("scalar", s_a, lambda e: e.activation(
            stat_q[0:1, 1024:1025], p3[0:1, 0:1], AF.Identity,
            bias=0.0, scale=1.0 / 2048))
        W("vector", s_a, t_statev)
        E("vector", lambda e: e.tensor_tensor(
            stat_t[0:1, :], stat_m[0:1, :], stat_m[0:1, :], ALU.mult))
        E("vector", lambda e: e.tensor_tensor(
            stat_t[0:1, :], stat_q[0:1, :], stat_t[0:1, :], ALU.subtract))
        t_var = OP("vector", s_v, lambda e: e.tensor_scalar_add(
            stat_t[0:1, :], stat_t[0:1, :], 1e-5))
        W("scalar", s_v, t_var)
        E("scalar", lambda e: e.activation(stat_t[0:1, :], stat_t[0:1, :], AF.Ln))
        t_rstd = OP("scalar", s_a, lambda e: e.activation(
            stat_t[0:1, :], stat_t[0:1, :], AF.Exp, bias=0.0, scale=-0.5))
        W("vector", s_a, t_rstd)
        t_mr = OP("vector", s_v, lambda e: e.tensor_tensor(
            stat_m[0:1, :], stat_m[0:1, :], stat_t[0:1, :], ALU.mult))

        W("tensor", s_v, t_mr)
        W("tensor", s_a, t_statev)
        for reg, src in [
            (p0[:, 0:512], stat_t[0:1, 0:512]),
            (p0[:, 512:1024], stat_t[0:1, 512:1024]),
            (p1[:, 0:1], stat_t[0:1, 1024:1025]),
            (p2[:, 0:512], stat_m[0:1, 0:512]),
            (p2[:, 512:1024], stat_m[0:1, 512:1024]),
        ]:
            E("tensor", lambda e, reg=reg, src=src: e.matmul(
                reg, onesr[:, :], src, start=True, stop=True,
                skip_group_check=True))
        t_mmrep = OP("tensor", s_mm, lambda e: e.matmul(
            p3[:, 0:1], onesr[:, :], stat_m[0:1, 1024:1025],
            start=True, stop=True, skip_group_check=True))
        W("scalar", s_mm, t_mmrep)
        E("scalar", lambda e: e.activation(smm[:, 0:1024], p0[:, 0:1024],
                                           AF.Identity, bias=0.0, scale=1.0))
        E("scalar", lambda e: e.activation(smm[:, 1024:1025], p1[:, 0:1],
                                           AF.Identity, bias=0.0, scale=1.0))
        E("scalar", lambda e: e.activation(smm[:, 1025:2049], p2[:, 0:1024],
                                           AF.Identity, bias=0.0, scale=1.0))
        t_rep = OP("scalar", s_a, lambda e: e.activation(
            smm[:, 2049:2050], p3[:, 0:1], AF.Identity, bias=0.0, scale=1.0))
        rstd_rep = smm[:, 0:TPC + 1]
        mr_rep = smm[:, TPC + 1:2 * (TPC + 1)]

        t_xn = {}
        for g in range(G):
            if g >= 2:
                W("gpsimd", s_v, t_xn[g - 2])
            else:
                W("gpsimd", s_mm, t_mmstat[G - 2 + g])
            tl = DMA("gpsimd", s_g, Xst[g % 2][:, :], xT[g * P:(g + 1) * P, :])
            W("vector", s_g, tl)
            if g == 0:
                W("vector", s_a, t_rep)
            E("vector", lambda e, g=g: e.tensor_tensor(
                Xst[g % 2][:, :], Xst[g % 2][:, :], rstd_rep, ALU.mult))
            t_xn[g] = OP("vector", s_v, lambda e, g=g: e.tensor_tensor(
                xn[:, g * XST + 1:(g + 1) * XST], Xst[g % 2][:, :],
                mr_rep, ALU.subtract))

        def lerp(dst_fn, tm_name, wait_tok=None):
            toks = {}
            for g in range(G):
                E("vector", lambda e, g=g: e.tensor_scalar_mul(
                    dst_fn(g), xn_own(g), vcol(tm_name, g)))
                toks[g] = OP("vector", s_v, lambda e, g=g: e.scalar_tensor_tensor(
                    dst_fn(g), xn_prev(g), d_om[tm_name][:, g:g + 1], dst_fn(g),
                    ALU.mult, ALU.add))
            return toks

        t_xk = lerp(b1t, "tmk")

        def proj_mms(i, ps, rhs_fn, tok_sem=True):
            t = None
            for dc in range(G):
                st, sp = (dc == 0), (dc == G - 1)
                for half in range(2):
                    fn = (lambda e, i=i, dc=dc, st=st, sp=sp, ps=ps, half=half:
                          e.matmul(ps[:, half * 512:(half + 1) * 512],
                                   wb[i % 4][:, dc * 128:(dc + 1) * 128],
                                   rhs_fn(dc, half), start=st, stop=sp,
                                   skip_group_check=True))
                    if dc == G - 1 and half == 1:
                        t = OP("tensor", s_mm, fn)
                    else:
                        E("tensor", fn)
            return t

        def b1_rhs(dc, half):
            return b1[:, dc * TPC + half * 512:dc * TPC + (half + 1) * 512]

        def bigR_rhs(dc, half):
            return bigb[:, (G + dc) * TPC + half * 512:
                        (G + dc) * TPC + (half + 1) * 512]

        t_kmm = {}
        t_kev = {}
        pend = {}
        pend[0] = ws.load(wk[0])
        ws.cast(pend[0])
        for ct in range(G):
            if ct + 1 < G:
                pend[ct + 1] = ws.load(wk[ct + 1])
                ws.cast(pend[ct + 1])
            i = pend[ct]
            if ct >= 1:
                W("scalar", s_mm, t_kmm[ct - 1])
                t_kev[ct - 1] = OP("scalar", s_a, lambda e, ct=ct - 1: e.activation(
                    bigL(ct), PS[ct % 4][:, :], AF.Exp,
                    bias=vcol("bk", ct), scale=1.0))
            W("tensor", s_a, ws.cast_tok[i])
            if ct == 0:
                W("tensor", s_v, t_xk[G - 1])
                W("tensor", s_a, t_rep)
            if ct >= 4:
                W("tensor", s_a, t_kev[ct - 4])
            t_kmm[ct] = proj_mms(i, PS[ct % 4], b1_rhs)
            ws.consumed(i, t_kmm[ct])
        W("scalar", s_mm, t_kmm[G - 1])
        t_kev[G - 1] = OP("scalar", s_a, lambda e, ct=G - 1: e.activation(
            bigL(ct), PS[ct % 4][:, :], AF.Exp, bias=vcol("bk", ct), scale=1.0))

        t_xv = lerp(bigR, "tmv")

        SA2 = ssa[:, 0:TPC]
        SB2 = ssa[:, TPC:2 * TPC]
        SCR = smm[:, 0:TPC]
        lam_rep = smm[:, TPC:2 * TPC]

        vtmp = [smm[:, 0:TPC], smm[:, TPC:2 * TPC]]
        t_vmm = {}
        t_vev = {}
        t_ekv = {}
        t_p1 = None
        pend = {}
        pend[0] = ws.load(wv[0])
        ws.cast(pend[0])
        for ct in range(G):
            if ct + 1 < G:
                pend[ct + 1] = ws.load(wv[ct + 1])
                ws.cast(pend[ct + 1])
            i = pend[ct]
            if ct >= 1:
                c = ct - 1
                W("scalar", s_mm, t_vmm[c])
                if c == 0:
                    W("scalar", s_v, t_xn[G - 1])
                if c >= 2:
                    W("scalar", s_v, t_ekv[c - 2])
                t_vev[c] = OP("scalar", s_a, lambda e, c=c: e.activation(
                    vtmp[c % 2][:, :], PS[c % 4][:, :], AF.Identity,
                    bias=vcol("bv", c), scale=1.0))
                W("vector", s_a, t_vev[c])
                if c == 0:
                    W("vector", s_mm, t_kmm[G - 1])
                t_ekv[c] = OP("vector", s_v, lambda e, c=c: e.tensor_tensor(
                    b1t(c), bigL(c), vtmp[c % 2][:, :], ALU.mult))
                if c == 0:
                    W("vector", s_a, t_dvec)
                E("vector", lambda e, c=c: e.tensor_scalar_mul(
                    lamp1[:, :], ones1k[:, :], d_lam[:, c:c + 1]))
                E("vector", lambda e, c=c: e.tensor_tensor_scan(
                    SA2, lamp1[:, :], b1t(c), 0.0, ALU.mult, ALU.add))
                E("vector", lambda e, c=c: e.tensor_tensor_scan(
                    SB2, lamp1[:, :], bigL(c), 0.0, ALU.mult, ALU.add))
                E("vector", lambda e: e.drain())
                E("vector", lambda e, c=c: e.tensor_copy(
                    lae[:, c:c + 1], SA2[:, TPC - 1:TPC]))
                t_p1 = OP("vector", s_v, lambda e, c=c: e.tensor_copy(
                    lae[:, 16 + c:17 + c], SB2[:, TPC - 1:TPC]))
            W("tensor", s_a, ws.cast_tok[i])
            if ct == 0:
                W("tensor", s_v, t_xv[G - 1])
                W("tensor", s_a, t_kev[G - 1])
            if ct >= 4:
                W("tensor", s_a, t_vev[ct - 4])
            t_vmm[ct] = proj_mms(i, PS[ct % 4], bigR_rhs)
            ws.consumed(i, t_vmm[ct])
        c = G - 1
        W("scalar", s_mm, t_vmm[c])
        W("scalar", s_v, t_ekv[c - 2])
        t_vev[c] = OP("scalar", s_a, lambda e, c=c: e.activation(
            vtmp[c % 2][:, :], PS[c % 4][:, :], AF.Identity,
            bias=vcol("bv", c), scale=1.0))
        W("vector", s_a, t_vev[c])
        t_ekv[c] = OP("vector", s_v, lambda e, c=c: e.tensor_tensor(
            b1t(c), bigL(c), vtmp[c % 2][:, :], ALU.mult))
        E("vector", lambda e, c=c: e.tensor_scalar_mul(
            lamp1[:, :], ones1k[:, :], d_lam[:, c:c + 1]))
        E("vector", lambda e, c=c: e.tensor_tensor_scan(
            SA2, lamp1[:, :], b1t(c), 0.0, ALU.mult, ALU.add))
        E("vector", lambda e, c=c: e.tensor_tensor_scan(
            SB2, lamp1[:, :], bigL(c), 0.0, ALU.mult, ALU.add))
        E("vector", lambda e: e.drain())
        E("vector", lambda e, c=c: e.tensor_copy(
            lae[:, c:c + 1], SA2[:, TPC - 1:TPC]))
        t_p1 = OP("vector", s_v, lambda e, c=c: e.tensor_copy(
            lae[:, 16 + c:17 + c], SB2[:, TPC - 1:TPC]))

        W("gpsimd", s_v, t_p1)
        DMA("gpsimd", s_g, cc_in[0], lae[:, 0:16])
        t_ccin = DMA("gpsimd", s_g, cc_in[1], lae[:, 16:32])
        W("gpsimd", s_g, t_ccin)
        t_ccdone = OP("gpsimd", s_cc, lambda e: e.collective_compute(
            "AllGather", ALU.bypass, replica_groups=[list(range(NCORES))],
            ins=[cc_in.ap().opt()], outs=[cc_out.ap().opt()]))
        W("gpsimd", s_cc, t_ccdone)
        t_gab = None
        for c in range(NCORES):
            DMA("gpsimd", s_g, gab[:, c * 16:(c + 1) * 16], cc_out[c, 0])
            t_gab = DMA("gpsimd", s_g, gab[:, 128 + c * 16:128 + (c + 1) * 16],
                        cc_out[c, 1])

        W("vector", s_mm, t_vmm[G - 1])
        t_xr = lerp(bigR, "tmr")

        W("vector", s_g, t_gab)
        W("vector", s_g, t_cpow)
        for c in range(NCORES - 1):
            E("vector", lambda e, c=c: e.tensor_scalar_mul(
                ctab[:, c * 16:(c + 1) * 16], d_decay, cpow_s[:, c:c + 1]))
        t_ctab = OP("vector", s_v, lambda e, c=NCORES - 1: e.tensor_scalar_mul(
            ctab[:, c * 16:(c + 1) * 16], d_decay, cpow_s[:, c:c + 1]))
        W("scalar", s_v, t_ctab)
        t_cexp = OP("scalar", s_a,
                    lambda e: e.activation(ctab[:, :], ctab[:, :], AF.Exp))
        W("vector", s_a, t_cexp)
        E("vector", lambda e: e.tensor_tensor(
            gab[:, 0:128], ctab[:, :], gab[:, 0:128], ALU.mult))
        E("vector", lambda e: e.tensor_tensor(
            gab[:, 128:256], ctab[:, :], gab[:, 128:256], ALU.mult))
        E("vector", lambda e: e.drain())
        E("vector", lambda e: e.tensor_reduce(
            lae[:, 32:48],
            gab[:, 0:128].rearrange("p (c g) -> p g c", c=8, g=16),
            mybir.AxisListType.X, ALU.add))
        E("vector", lambda e: e.drain())
        t_astart = OP("vector", s_v, lambda e: e.tensor_reduce(
            lae[:, 48:64],
            gab[:, 128:256].rearrange("p (c g) -> p g c", c=8, g=16),
            mybir.AxisListType.X, ALU.add))


        stats_ctx.close()
        sgb = ent(nc.sbuf_tensor("sgb", [P, 2048], BF16))

        t_rmm = {}
        t_nd = {}
        t_recip = {}
        t_wkv = {}
        t_sig = {}
        t_rwkv = {}
        pend = {}
        pend[0] = ws.load(wr[0])
        ws.cast(pend[0])
        for g in range(G):
            if g + 1 < G:
                pend[g + 1] = ws.load(wr[g + 1])
                ws.cast(pend[g + 1])
            i = pend[g]
            W("tensor", s_a, ws.cast_tok[i])
            if g == 0:
                W("tensor", s_v, t_xr[G - 1])
                W("tensor", s_a, t_vev[G - 1])
            if g >= 4:
                W("tensor", s_a, t_sig[g - 4])
            t_rmm[g] = proj_mms(i, PS[g % 4], bigR_rhs)
            ws.consumed(i, t_rmm[g])

            if g == 0:
                W("vector", s_v, t_astart)
            E("vector", lambda e, g=g: e.tensor_scalar_mul(
                lam_rep, ones1k[:, :], d_lam[:, g:g + 1]))
            E("vector", lambda e, g=g: e.tensor_tensor_scan(
                SCR, lam_rep, b1t(g), lae[:, 32 + g:33 + g],
                ALU.mult, ALU.add))
            if g >= 1:
                W("vector", s_gp, t_wkv[g - 1])
            E("vector", lambda e, g=g: e.scalar_tensor_tensor(
                SA2[:, 1:TPC], b1t(g)[:, 1:TPC], d_cu[:, g:g + 1],
                SCR[:, 0:TPC - 1], ALU.mult, ALU.add))
            E("vector", lambda e, g=g: e.scalar_tensor_tensor(
                SA2[:, 0:1], b1t(g)[:, 0:1], d_cu[:, g:g + 1],
                lae[:, 32 + g:33 + g], ALU.mult, ALU.add))
            E("vector", lambda e, g=g: e.tensor_tensor_scan(
                SCR, lam_rep, bigL(g), lae[:, 48 + g:49 + g],
                ALU.mult, ALU.add))
            E("vector", lambda e, g=g: e.scalar_tensor_tensor(
                SB2[:, 1:TPC], bigL(g)[:, 1:TPC], d_cu[:, g:g + 1],
                SCR[:, 0:TPC - 1], ALU.mult, ALU.add))
            t_nd[g] = OP("vector", s_v, lambda e, g=g: e.scalar_tensor_tensor(
                SB2[:, 0:1], bigL(g)[:, 0:1], d_cu[:, g:g + 1],
                lae[:, 48 + g:49 + g], ALU.mult, ALU.add))
            W("scalar", s_v, t_nd[g])
            E("scalar", lambda e: e.activation(SB2, SB2, AF.Ln))
            t_recip[g] = OP("scalar", s_a, lambda e: e.activation(
                SB2, SB2, AF.Exp, bias=0.0, scale=-1.0))
            W("gpsimd", s_a, t_recip[g])
            t_wkv[g] = OP("gpsimd", s_gp, lambda e, g=g: e.tensor_tensor(
                bigL(g), SA2, SB2, ALU.mult))
            W("scalar", s_mm, t_rmm[g])
            if g >= 2:
                W("scalar", s_gp, t_rwkv[g - 2])
            t_sig[g] = OP("scalar", s_a, lambda e, g=g: e.activation(
                sgb[:, (g % 2) * 1024:(g % 2 + 1) * 1024], PS[g % 4][:, :],
                AF.Sigmoid, bias=vcol("br", g), scale=1.0))
            W("gpsimd", s_a, t_sig[g])
            t_rwkv[g] = OP("gpsimd", s_gp, lambda e, g=g: e.tensor_tensor(
                bigL(g), sgb[:, (g % 2) * 1024:(g % 2 + 1) * 1024],
                bigL(g), ALU.mult))

        t_womm = {}
        t_woev = {}

        def wo_evac(dt):
            return lambda e, dt=dt: e.activation(
                b1t(dt), PS[dt % 4][:, :], AF.Identity,
                bias=vcol("bo", dt), scale=1.0)

        pend = {}
        pend[0] = ws.load(wo[0])
        ws.cast(pend[0])
        for dt in range(G):
            if dt + 1 < G:
                pend[dt + 1] = ws.load(wo[dt + 1])
                ws.cast(pend[dt + 1])
            i = pend[dt]
            if dt >= 1:
                W("scalar", s_mm, t_womm[dt - 1])
                if dt == 1:
                    W("scalar", s_v, t_nd[G - 1])
                t_woev[dt - 1] = OP("scalar", s_a, wo_evac(dt - 1))
            W("tensor", s_a, ws.cast_tok[i])
            if dt == 0:
                W("tensor", s_gp, t_rwkv[G - 1])
                W("tensor", s_a, t_sig[G - 1])
            if dt >= 4:
                W("tensor", s_a, t_woev[dt - 4])
            t = None
            for cc in range(G):
                st, sp = (cc == 0), (cc == G - 1)
                for half in range(2):
                    fn = (lambda e, i=i, cc=cc, st=st, sp=sp, dt=dt, half=half:
                          e.matmul(PS[dt % 4][:, half * 512:(half + 1) * 512],
                                   wb[i % 4][:, cc * 128:(cc + 1) * 128],
                                   bigb[:, cc * TPC + half * 512:
                                        cc * TPC + (half + 1) * 512],
                                   start=st, stop=sp, skip_group_check=True))
                    if cc == G - 1 and half == 1:
                        t = OP("tensor", s_mm, fn)
                    else:
                        E("tensor", fn)
            t_womm[dt] = t
            ws.consumed(i, t)
        W("scalar", s_mm, t_womm[G - 1])
        t_woev[G - 1] = OP("scalar", s_a, wo_evac(G - 1))

        t_st_l = {}
        t_oev = {}
        t_w2mm_last = None
        t_hid_prev = None

        def bf_stage_pieces(h, lo_tile, n_tiles):
            nb = n_tiles // (2 * h)
            seg = bigb[:, lo_tile * 512:(lo_tile + n_tiles) * 512]
            ap5 = seg.rearrange("p (b two i t) -> p b two i t",
                                b=nb, two=2, i=h, t=512)
            out = []
            if h == 1:
                step = 2 if nb >= 2 else 1
                for b0 in range(0, nb, step):
                    bs = min(step, nb - b0)
                    out.append((ap5[:, b0:b0 + bs, 0], ap5[:, b0:b0 + bs, 1],
                                (bs, 1, 512), b0 == 0))
            else:
                ic = 2 if h >= 2 else 1
                for b in range(nb):
                    for i0 in range(0, h, ic):
                        out.append((ap5[:, b:b + 1, 0, i0:i0 + ic],
                                    ap5[:, b:b + 1, 1, i0:i0 + ic],
                                    (1, ic, 512), b == 0 and i0 == 0))
            return out

        bf_state = {"j": 0, "cp": {}, "last_cp": 0}

        def bf_emit(pieces_list):
            for (A, B, shp, sstart) in pieces_list:
                j = bf_state["j"]
                bf_state["j"] = j + 1
                half = sgb[:, (j % 2) * 1024:(j % 2 + 1) * 1024]
                halfr = half.rearrange("p (b i t) -> p b i t",
                                       b=shp[0], i=shp[1], t=shp[2])
                if sstart and j >= 1:
                    W("vector", s_a, bf_state["cp"][j - 1])
                elif j >= 2:
                    W("vector", s_a, bf_state["cp"][j - 2])
                E("vector", lambda e, A=A, B=B, hr=halfr: e.tensor_tensor(
                    hr, A, B, ALU.subtract))
                ta = OP("vector", s_v, lambda e, A=A, B=B: e.tensor_tensor(
                    A, A, B, ALU.add))
                W("scalar", s_v, ta)
                tcp = OP("scalar", s_a, lambda e, B=B, hr=halfr: e.copy(B, hr))
                bf_state["cp"][j] = tcp
                bf_state["last_cp"] = tcp

        for tc in range(2):
            bf_queue = []
            t_ffmm = {}
            t_hid = {}
            t_silu = {}
            pendf = {}
            a0p = (ws.load(w1[0]), ws.load(w3[0]))
            ws.cast(a0p[0])
            ws.cast(a0p[1])
            pendf[0] = a0p
            for a in range(GA):
                if a + 1 < GA:
                    n1 = ws.load(w1[a + 1])
                    ws.cast(n1)
                    n3 = ws.load(w3[a + 1])
                    ws.cast(n3)
                    pendf[a + 1] = (n1, n3)
                i1, i3 = pendf[a]
                W("tensor", s_a, ws.cast_tok[i3])
                W("tensor", s_a, ws.cast_tok[i1])
                if a == 0 and tc == 0:
                    W("tensor", s_a, t_woev[G - 1])
                if a >= 4:
                    W("tensor", s_v, t_hid[a - 4])
                elif tc == 1:
                    W("tensor", s_v, t_hid_prev[GA - 4 + a])
                ps = PS[a % 4]
                t = None
                for dc in range(G):
                    st, sp = (dc == 0), (dc == G - 1)
                    rhs = xn[:, dc * XST + 2 + tc * 512:
                             dc * XST + 2 + tc * 512 + 512]
                    E("tensor", lambda e, i=i1, st=st, sp=sp, ps=ps, rhs=rhs, dc=dc:
                      e.matmul(ps[:, 0:512], wb[i % 4][:, dc * 128:(dc + 1) * 128],
                               rhs, start=st, stop=sp, skip_group_check=True))
                    fn = (lambda e, i=i3, st=st, sp=sp, ps=ps, rhs=rhs, dc=dc:
                          e.matmul(ps[:, 512:1024],
                                   wb[i % 4][:, dc * 128:(dc + 1) * 128],
                                   rhs, start=st, stop=sp, skip_group_check=True))
                    if dc == G - 1:
                        t = OP("tensor", s_mm, fn)
                    else:
                        E("tensor", fn)
                t_ffmm[a] = t
                ws.consumed(i1, t)
                ws.consumed(i3, t)
                W("scalar", s_mm, t_ffmm[a])
                if tc == 0 and a == 0:
                    W("scalar", s_v, t_nd[G - 1])
                if a >= 2:
                    W("scalar", s_v, t_hid[a - 2])
                t_silu[a] = OP("scalar", s_a, lambda e, a=a, ps=ps: e.activation(
                    smm[:, (a % 2) * 512:(a % 2 + 1) * 512], ps[:, 0:512], AF.Silu))
                W("vector", s_a, t_silu[a])
                if tc == 0 and a == 0:
                    W("vector", s_v, t_nd[G - 1])
                    W("vector", s_mm, t_womm[G - 1])
                if tc == 1 and a == 0:
                    W("vector", s_mm, t_w2mm_last)
                t_hid[a] = OP("vector", s_v, lambda e, a=a, ps=ps: e.tensor_tensor(
                    bigb[:, a * 512:(a + 1) * 512],
                    smm[:, (a % 2) * 512:(a % 2 + 1) * 512],
                    ps[:, 512:1024], ALU.mult))
                if a % 16 == 15:
                    gb = a // 16
                    for h in (1, 2, 4, 8):
                        bf_queue.extend(bf_stage_pieces(h, 16 * gb, 16))
                    if a % 32 == 31:
                        bf_queue.extend(bf_stage_pieces(16, 32 * (a // 32), 32))
                for _ in range(2):
                    if bf_queue:
                        bf_emit([bf_queue.pop(0)])
            bf_queue.extend(bf_stage_pieces(32, 0, 64))
            while bf_queue:
                bf_emit([bf_queue.pop(0)])
            t_hid_prev = t_hid
            t_bfly = bf_state["last_cp"]

            t_hmm = {}
            t_zev = {}
            for a in range(GA):
                if a == 0:
                    W("tensor", s_a, t_bfly)
                    W("tensor", s_g, t_h128)
                if a >= 2:
                    W("tensor", s_a, t_zev[a - 2])
                reg = p2[:, (a % 2) * 512:(a % 2 + 1) * 512]
                t_hmm[a] = OP("tensor", s_mm, lambda e, a=a, reg=reg: e.matmul(
                    reg, h128_s[:, :], bigb[:, a * 512:(a + 1) * 512],
                    start=True, stop=True, skip_group_check=True))
                W("scalar", s_mm, t_hmm[a])
                t_zev[a] = OP("scalar", s_a, lambda e, a=a, reg=reg: e.activation(
                    bigb[:, a * 512:(a + 1) * 512], reg, AF.Copy,
                    bias=0.0, scale=S8192))

            for dt in range(G):
                xsl = smm[:, 1024 + (dt % 2) * 512:1024 + (dt % 2 + 1) * 512]
                prev = None
                if dt >= 2:
                    prev = (tc, dt - 2)
                elif tc == 1:
                    prev = (0, G - 2 + dt)
                if prev is not None:
                    W("gpsimd", s_v, t_oev[prev])
                elif tc == 0:
                    W("gpsimd", s_v, t_nd[G - 1])
                t_xsl = DMA("gpsimd", s_g, xsl,
                            xT[dt * P:(dt + 1) * P,
                               1 + tc * 512:1 + tc * 512 + 512])
                t_w2mm = None
                for blk in range(4):
                    i = ws.load(w2[dt * 4 + blk])
                    ws.cast(i)
                    W("tensor", s_a, ws.cast_tok[i])
                    if blk == 0:
                        if dt == 0:
                            W("tensor", s_a, t_zev[GA - 1])
                        if prev is not None:
                            W("tensor", s_v, t_oev[prev])
                    reg = p3[:, (dt % 2) * 512:(dt % 2 + 1) * 512]
                    tb = None
                    for b16 in range(16):
                        b = blk * 16 + b16
                        st, sp = (b == 0), (b == GA - 1)
                        fn = (lambda e, i=i, b=b, b16=b16, st=st, sp=sp, reg=reg:
                              e.matmul(reg, wb[i % 4][:, b16 * 128:(b16 + 1) * 128],
                                       bigb[:, b * 512:(b + 1) * 512],
                                       start=st, stop=sp, skip_group_check=True))
                        if b16 == 15:
                            tb = OP("tensor", s_mm, fn)
                        else:
                            E("tensor", fn)
                    ws.consumed(i, tb)
                    if blk == 3:
                        t_w2mm = tb
                t_w2mm_last = t_w2mm
                evst = ssa[:, (dt % 2) * 512:(dt % 2 + 1) * 512]
                W("vector", s_mm, t_w2mm)
                W("vector", s_g, t_xsl)
                if prev is not None:
                    W("vector", s_st, t_st_l[prev])
                E("vector", lambda e, dt=dt, tc=tc, evst=evst: e.tensor_tensor(
                    evst, p3[:, (dt % 2) * 512:(dt % 2 + 1) * 512],
                    b1[:, dt * TPC + tc * 512:dt * TPC + tc * 512 + 512],
                    ALU.add))
                t_oev[(tc, dt)] = OP("vector", s_v, lambda e, evst=evst, xsl=xsl:
                                     e.tensor_tensor(evst, evst, xsl, ALU.add))
                W("gpsimd", s_v, t_oev[(tc, dt)])
                t_st_l[(tc, dt)] = DMA(
                    "gpsimd", s_st,
                    out_fm[dt * P:(dt + 1) * P, tc * 512:(tc + 1) * 512], evst)

        W("gpsimd", s_st, s_st.n)
        W("sync", s_st, s_st.n)

        block = ent(nc.Block())

        @block.sync
        def _(e):
            for f in EP["sync"]:
                f(e)

        @block.tensor
        def _(e):
            for f in EP["tensor"]:
                f(e)

        @block.vector
        def _(e):
            for f in EP["vector"]:
                f(e)

        @block.scalar
        def _(e):
            for f in EP["scalar"]:
                f(e)

        @block.gpsimd
        def _(e):
            for f in EP["gpsimd"]:
                f(e)

    return nc


def _tile_proj(wm):
    o, h = wm.shape
    return np.ascontiguousarray(
        wm.reshape(o // 128, 128, h // 128, 128).transpose(0, 3, 2, 1)
    ).reshape(o // 128, 128, h)


def _tile_w2(wm):
    return np.ascontiguousarray(
        wm.reshape(16, 128, 4, 16, 128).transpose(0, 2, 4, 3, 1)
    ).reshape(64, 128, 2048)


def _col16(v):
    return v.reshape(16, 128).T


def kernel(L0_attn_x, L0_attn_Wk, L0_attn_bk, L0_attn_Wv, L0_attn_bv,
           L0_attn_Wr, L0_attn_br, L0_attn_Wo, L0_attn_bo, L0_attn_tmk,
           L0_attn_tmv, L0_attn_tmr, L0_attn_time_decay, L0_attn_time_first,
           L0_ffn_W1, L0_ffn_W2, L0_ffn_W3):
    x = np.asarray(L0_attn_x, dtype=np.float32)
    B, T, D = x.shape
    xall = np.ascontiguousarray(x.reshape(B * T, D))

    wk_t = _tile_proj(np.asarray(L0_attn_Wk, np.float32))
    wv_t = _tile_proj(np.asarray(L0_attn_Wv, np.float32))
    wr_t = _tile_proj(np.asarray(L0_attn_Wr, np.float32))
    wo_t = _tile_proj(np.asarray(L0_attn_Wo, np.float32))
    w1_t = _tile_proj(np.asarray(L0_ffn_W1, np.float32))
    w3_t = _tile_proj(np.asarray(L0_ffn_W3, np.float32))
    w2_t = _tile_w2(np.asarray(L0_ffn_W2, np.float32))

    vv = np.concatenate([
        _col16(np.asarray(a, np.float32).reshape(-1)) for a in
        [L0_attn_tmk, L0_attn_tmv, L0_attn_tmr, L0_attn_bk, L0_attn_bv,
         L0_attn_br, L0_attn_time_decay, L0_attn_time_first, L0_attn_bo]
    ], axis=1)
    vv = np.ascontiguousarray(vv)

    H = np.array([[1.0]], np.float32)
    for _ in range(7):
        H = np.block([[H, H], [H, -H]])
    h128_a = H.astype(ml_dtypes.bfloat16)

    in_maps = []
    qper = (B * T) // NCORES
    qps = T // qper
    for c in range(NCORES):
        r0 = c * qper
        q = c % qps
        prev = np.zeros((1, D), np.float32) if q == 0 else xall[r0 - 1:r0]
        blk = np.concatenate([prev, xall[r0:r0 + qper]], axis=0)
        xT_c = np.ascontiguousarray(blk.T)
        cp = np.full(NCORES, 1e9, np.float32)
        for ci in range(NCORES):
            if (ci // qps) == (c // qps) and (ci % qps) < q:
                cp[ci] = 1024.0 * (q - (ci % qps) - 1)
        cpow_c = np.ascontiguousarray(
            np.broadcast_to(cp[None, :], (P, NCORES)).astype(np.float32))
        in_maps.append(dict(
            xT=xT_c, wk=wk_t, wv=wv_t, wr=wr_t, wo=wo_t,
            w1=w1_t, w3=w3_t, w2=w2_t, vecs=vv, cpow=cpow_c, h128=h128_a))

    nc = build()
    res = run_bass_kernel_spmd(nc, in_maps, core_ids=list(range(NCORES)))
    out = np.empty((B * T, D), np.float32)
    for c in range(NCORES):
        out[c * qper:(c + 1) * qper] = res.results[c]["out_fm"].T
    return out.reshape(B, T, D)


if __name__ == "__main__":
    import reference
    inp = {k: np.asarray(v) for k, v in reference.setup_inputs().items()}
    got = kernel(**inp)
    exp = np.asarray(reference.reference(**inp))
    num = float(((got - exp) ** 2).mean())
    den = float((exp ** 2).mean())
    print("resvar", num / den)
